# version 01
# baseline (speedup 1.0000x reference)
import contextlib
from collections import defaultdict

import numpy as np
import concourse.bass as bass
import concourse.mybir as mybir

F32 = mybir.dt.float32
BF16 = mybir.dt.bfloat16
AF = mybir.ActivationFunctionType
ALU = mybir.AluOpType
AX = mybir.AxisListType

D = 1024
NST = 64
TOWN = 2048
TFULL = 4096
Q = 64
NDC = 8
E = 8
DFF = 4096
EPS_LN = 1e-5
EPS_TOP = float(np.float32(1.1920929e-07))

PHASES = ["ln_win", "conv_params", "bloc", "local", "router", "wout", "moe", "store"]


def build(debug_keys=(), upto="store", ln_sub="full"):
    stop = PHASES.index(upto)
    nc = bass.Bass("TRN2", target_bir_lowering=False)
    es = contextlib.ExitStack()
    t = {}
    cnt = defaultdict(int)
    debug_keys = list(debug_keys)

    def dri(name, shape, dtype=F32):
        t[name] = nc.dram_tensor(name, list(shape), dtype, kind="ExternalInput").ap()

    def sbt(name, shape, dtype):
        t[name] = es.enter_context(nc.sbuf_tensor(name, list(shape), dtype))

    dri("xf", (TFULL, D))
    dri("xoT", (D, TOWN))
    dri("cmask", (128, 1))
    dri("WinT", (D, D))
    dri("binb", (128, NDC))
    dri("WoutT", (D, D))
    dri("boutb", (128, NDC))
    dri("A_log_r", (128, NDC * NST))
    dri("B_r", (128, NDC * NST))
    dri("C_r", (128, NDC * NST))
    dri("dt_r", (128, NDC))
    dri("Dp_r", (128, NDC))
    dri("WrT_r", (128, NDC * E))
    dri("WgT", (E, D, DFF))
    dri("WuT", (E, D, DFF))
    dri("WdT", (E, DFF, D))
    dri("ident_c", (128, 128))
    dri("sel_c", (8, E * 128))
    dri("ones8_c", (8, 128))
    dri("ident8_c", (8, 8))
    dri("smalls_c", (128, 16))
    dri("iramp1_c", (128, 64))
    dri("mramp_c", (128, 64))
    t["out_tok"] = nc.dram_tensor("out_tok", [TOWN, D], F32, kind="ExternalOutput").ap()
    t["logdA_dram"] = nc.dram_tensor("logdA_dram", [D, NST], F32, kind="Internal").ap()

    sbt("xn_T", (128, NDC * TOWN), BF16)
    sbt("ident_f", (128, 128), F32)
    sbt("ident_bf", (128, 128), BF16)
    sbt("ones8", (8, 128), F32)
    sbt("ident8", (8, 8), F32)
    sbt("smalls", (128, 16), F32)
    sbt("iramp1", (128, 64), F32)
    sbt("mramp", (128, 64), F32)
    sbt("pvec", (128, 64), F32)
    sbt("WrT_sb", (128, NDC * E), F32)
    sbt("mvall", (128, 64), F32)
    sbt("rstd_all", (128, 32), F32)
    sbt("stats6", (128, 12), F32)
    sbt("k_loc", (128, NDC * Q), F32)
    sbt("logdA_P2", (128, 512), F32)
    sbt("Ct_P2", (128, 512), F32)
    sbt("r64_P2", (128, 512), F32)
    sbt("comb_T", (8, TOWN), BF16)
    sbt("wsum_sb", (8, 1), F32)
    sbt("wsum_diag", (8, 8), F32)
    sbt("wsum_bc", (128, 8), F32)

    sbt("AA", (128, 32768), BF16)
    sbt("AB", (128, 32768), BF16)
    sbt("AC", (128, 16384), BF16)

    def av(arena, byte_off, ncols, dtype):
        assert byte_off % 4 == 0
        o = byte_off // 2
        if dtype == BF16:
            return t[arena][:, o:o + ncols]
        return t[arena][:, o:o + 2 * ncols].bitcast(F32)

    xf_roll = lambda slot: av("AA", 0, 4096, F32)
    xn_tok = av("AA", 16384, 4096, BF16)
    xn_roll = lambda slot: av("AA", 24576 + slot * 8192, 4096, BF16)
    xps = lambda slot: av("AA", 40960 + slot * 1024, 512, BF16)
    Win_bf = av("AA", 43008, 8192, BF16)
    Alog_v = av("AA", 0, 512, F32)
    Bm_v = av("AA", 2048, 512, F32)
    Cm_v = av("AA", 4096, 512, F32)
    logdA_v = av("AA", 6144, 512, F32)
    Ct_v = av("AA", 8192, 512, F32)
    prodbuf = av("AA", 10240, 4096, F32)
    Pb_g = av("AA", 26624, 8192, BF16)
    CtP_g = av("AA", 43008, 4096, BF16)
    B_sb = av("AA", 51200, 1024, F32)
    S_g = av("AA", 55296, 4096, BF16)
    xpdt = av("AA", 0, 16384, BF16)
    y_dc = av("AA", 32768, 2048, F32)
    Wout_bf = av("AA", 40960, 8192, BF16)
    h_all = av("AA", 0, 16384, BF16)
    Wg_bf = lambda slot: av("AA", 32768 + slot * 8192, 4096, BF16)
    Wu_bf = lambda slot: av("AA", 49152 + slot * 8192, 4096, BF16)

    xpT = av("AB", 0, 32768, BF16)
    acc = av("AB", 0, 16384, F32)

    y_T = av("AC", 0, 16384, BF16)
    lraw = av("AC", 0, 2048, F32)[0:8, :]
    raw_tok = av("AC", 8192, 128, F32)
    nl_v = av("AC", 8704, 128, F32)
    ez_v = av("AC", 9216, 128, F32)
    s2_v = av("AC", 9728, 128, F32)
    msk1_v = av("AC", 10240, 128, F32)
    msk2_v = av("AC", 10752, 128, F32)
    t2_v = av("AC", 11264, 128, F32)
    comb_v = av("AC", 11776, 128, F32)
    rm_v = av("AC", 12288, 96, F32)
    Wd_bf = av("AC", 0, 8192, BF16)
    comb_bc = av("AC", 16384, 2048, BF16)
    gr_sb = lambda slot: av("AC", 20480 + slot * 1024, 512, BF16)
    out_stg = av("AC", 22528, 1024, F32)
    sel_bf = av("AC", 26624, 1024, BF16)[0:8, :]

    sems = {}
    for s in ["LDX", "LDW", "T", "V", "S", "G", "ST"]:
        sems[s] = es.enter_context(nc.semaphore(s))
    block = es.enter_context(nc.Block())

    def inc(instr, s, n=1):
        cnt[s] += n
        return instr.then_inc(sems[s], n)

    def cur(s):
        return cnt[s]

    M = {}
    PV = {"bin": 0, "bout": 8, "dt": 16, "Dp": 24, "dpk0": 32, "cm": 40}

    def pv(name, j=0):
        o = PV[name] + j
        return t["pvec"][:, o:o + 1]

    ONES = t["smalls"][:, 0:1]
    JNEG = t["smalls"][:, 1:2]

    @block.sync
    def _(sy):
        for dst, src in [("ident_f", "ident_c"), ("ones8", "ones8_c"), ("ident8", "ident8_c"),
                         ("smalls", "smalls_c"), ("iramp1", "iramp1_c"), ("mramp", "mramp_c"),
                         ("WrT_sb", "WrT_r")]:
            inc(sy.dma_start(out=t[dst][:], in_=t[src]), "LDX", 16)
        inc(sy.dma_start(out=t["pvec"][:, 0:8], in_=t["binb"]), "LDX", 16)
        inc(sy.dma_start(out=t["pvec"][:, 8:16], in_=t["boutb"]), "LDX", 16)
        inc(sy.dma_start(out=t["pvec"][:, 16:24], in_=t["dt_r"]), "LDX", 16)
        inc(sy.dma_start(out=t["pvec"][:, 24:32], in_=t["Dp_r"]), "LDX", 16)
        inc(sy.dma_start(out=t["pvec"][:, 40:41], in_=t["cmask"]), "LDX", 16)
    M["LDX_SMALL"] = cur("LDX")

    @block.gpsimd
    def _(g):
        inc(g.dma_start(out=t["ident_bf"][:], in_=t["ident_c"]), "LDW", 16)
        inc(g.dma_start(out=Win_bf.rearrange("p (dc o) -> p dc o", o=D),
                        in_=t["WinT"].rearrange("(dc p) o -> p dc o", p=128)), "LDW", 16)
    M["LDW_INIT"] = cur("LDW")

    ps_win_t = es.enter_context(nc.psum_tensor("ps_win", [128, 1024], F32))
    ps_win = [ps_win_t[:, 0:512], ps_win_t[:, 512:1024]]
    ps_tr_t = es.enter_context(nc.psum_tensor("ps_tr", [128, 2048], BF16))
    ps_tr = [ps_tr_t[:, 0:1024], ps_tr_t[:, 1024:2048]]

    vslot_free = {}
    xn_evac = {}
    xps_evac = {}
    xpt_evac = {}

    for grp in range(8):
        own = grp >= 4
        slot = 0

        @block.scalar
        def _(s, grp=grp):
            if grp >= 1:
                s.wait_ge(sems["V"], vslot_free[grp - 1])
            src = t["xf"].rearrange("(g q p) d -> g p q d", g=8, p=128)[grp]
            inc(s.dma_start(out=xf_roll(0).rearrange("p (q d) -> p q d", d=D), in_=src),
                "LDX", 16)

        @block.vector
        def _(v, grp=grp):
            v.wait_ge(sems["LDX"], cur("LDX"))
            xr = xf_roll(0)
            for q in range(4):
                tile = 4 * grp + q
                inc(v.bn_stats(t["stats6"][:, 0:6], xr[:, q * 1024:q * 1024 + 512]), "V")
                inc(v.bn_stats(t["stats6"][:, 6:12], xr[:, q * 1024 + 512:(q + 1) * 1024]), "V")
                v.drain()
                inc(v.bn_aggr(t["mvall"][:, 2 * tile:2 * tile + 2], t["stats6"][:]), "V")
                v.drain()
            vs = t["mvall"][:].rearrange("p (k two) -> p k two", two=2)[:, 4 * grp:4 * grp + 4, 1]
            inc(v.tensor_scalar(t["rstd_all"][:, 4 * grp:4 * grp + 4], vs, EPS_LN, None,
                                ALU.add, ALU.bypass), "V")
            v.drain()
            inc(v.reciprocal(t["rstd_all"][:, 4 * grp:4 * grp + 4],
                             t["rstd_all"][:, 4 * grp:4 * grp + 4]), "V")

        @block.scalar
        def _(s, grp=grp):
            s.wait_ge(sems["V"], cur("V"))
            inc(s.sqrt(t["rstd_all"][:, 4 * grp:4 * grp + 4],
                       t["rstd_all"][:, 4 * grp:4 * grp + 4]), "S")

        @block.vector
        def _(v, grp=grp):
            v.wait_ge(sems["S"], cur("S"))
            if grp >= 1 and ("XNTR_T", grp - 1) in M:
                v.wait_ge(sems["T"], M[("XNTR_T", grp - 1)])
            xr = xf_roll(0)
            for q in range(4):
                tile = 4 * grp + q
                inc(v.tensor_scalar(xn_tok[:, q * 1024:(q + 1) * 1024],
                                    xr[:, q * 1024:(q + 1) * 1024],
                                    t["mvall"][:, 2 * tile:2 * tile + 1],
                                    t["rstd_all"][:, tile:tile + 1],
                                    ALU.subtract, ALU.mult), "V")
        vslot_free[grp] = cur("V")
        M[("XN_TOK", grp)] = cur("V")

        for dc in range(NDC if ln_sub != "stats" else 0):
            bank = dc % 2

            @block.tensor
            def _(tt, grp=grp, dc=dc, bank=bank):
                tt.wait_ge(sems["V"], M[("XN_TOK", grp)])
                tt.wait_ge(sems["LDW"], M["LDW_INIT"])
                key = (grp, dc - 2) if dc >= 2 else ((grp - 1, dc + 6) if grp > 0 else None)
                if key is not None:
                    tt.wait_ge(sems["V"], xn_evac[key])
                for q in range(4):
                    inc(tt.transpose(ps_tr[bank][:, q * 128:(q + 1) * 128],
                                     xn_tok[:, q * 1024 + dc * 128:q * 1024 + (dc + 1) * 128],
                                     t["ident_bf"][:]), "T")

            @block.vector
            def _(v, grp=grp, dc=dc, bank=bank, own=own):
                v.wait_ge(sems["T"], cur("T"))
                if own:
                    o0 = dc * 2048 + (grp - 4) * 512
                    dst = t["xn_T"][:, o0:o0 + 512]
                else:
                    if grp >= 2:
                        v.wait_ge(sems["T"], M[("WIN_T", grp - 2)])
                    dst = xn_roll(grp % 2)[:, dc * 512:(dc + 1) * 512]
                inc(v.tensor_copy(dst, ps_tr[bank][:, 0:512]), "V")
                xn_evac[(grp, dc)] = cur("V")
        M[("XN_RDY", grp)] = cur("V")
        M[("XNTR_T", grp)] = cur("T")

        for doutc in range(NDC if ln_sub == "full" else 0):
            bank = doutc % 2

            @block.tensor
            def _(tt, grp=grp, doutc=doutc, bank=bank, own=own):
                tt.wait_ge(sems["V"], M[("XN_RDY", grp)])
                key = (grp, doutc - 2) if doutc >= 2 else ((grp - 1, doutc + 6) if grp > 0 else None)
                if key is not None:
                    tt.wait_ge(sems["S"], xps_evac[key])
                for dc in range(NDC):
                    if own:
                        mv = t["xn_T"][:, dc * 2048 + (grp - 4) * 512: dc * 2048 + (grp - 4) * 512 + 512]
                    else:
                        mv = xn_roll(grp % 2)[:, dc * 512:(dc + 1) * 512]
                    inc(tt.matmul(ps_win[bank],
                                  Win_bf[:, dc * D + doutc * 128: dc * D + (doutc + 1) * 128],
                                  mv, start=(dc == 0), stop=(dc == NDC - 1)), "T")

            @block.scalar
            def _(s, grp=grp, doutc=doutc, bank=bank):
                s.wait_ge(sems["T"], cur("T"))
                key = (grp, doutc - 2) if doutc >= 2 else ((grp - 1, doutc + 6) if grp > 0 else None)
                if key is not None:
                    s.wait_ge(sems["V"], xpt_evac[key])
                inc(s.activation(xps(bank), ps_win[bank], AF.Identity,
                                 bias=pv("bin", doutc)), "S")
                xps_evac[(grp, doutc)] = cur("S")

            @block.tensor
            def _(tt, grp=grp, doutc=doutc, bank=bank):
                tt.wait_ge(sems["S"], cur("S"))
                for q in range(4):
                    inc(tt.transpose(ps_tr[bank][:, 512 + q * 128:512 + (q + 1) * 128],
                                     xps(bank)[:, q * 128:(q + 1) * 128],
                                     t["ident_bf"][:]), "T")

            @block.vector
            def _(v, grp=grp, doutc=doutc, bank=bank):
                v.wait_ge(sems["T"], cur("T"))
                dst = xpT.rearrange("p (c d) -> p c d", d=D)[
                    :, grp * 4:grp * 4 + 4, doutc * 128:(doutc + 1) * 128]
                src = ps_tr[bank][:, 512:1024].rearrange("p (q j) -> p q j", q=4)
                inc(v.tensor_copy(dst, src), "V")
                xpt_evac[(grp, doutc)] = cur("V")
        M[("WIN_T", grp)] = cur("T")

    @block.vector
    def _(v):
        v.wait_ge(sems["LDX"], M["LDX_SMALL"])
        ap = xpT[:, 0:16 * 1024]
        inc(v.tensor_scalar(ap, ap, pv("cm"), None, ALU.mult, ALU.bypass), "V")
    M["XPT_MASKED"] = cur("V")
    M["LN_T"] = cur("T")
    M["LN_V"] = cur("V")

    if stop >= PHASES.index("conv_params"):
        @block.sync
        def _(sy):
            sy.wait_ge(sems["T"], M["LN_T"])
            sy.wait_ge(sems["V"], M["LN_V"])
            inc(sy.dma_start(out=Alog_v, in_=t["A_log_r"]), "LDX", 16)
            inc(sy.dma_start(out=Bm_v, in_=t["B_r"]), "LDX", 16)
            inc(sy.dma_start(out=Cm_v, in_=t["C_r"]), "LDX", 16)
        M["CONVLD"] = cur("LDX")

        @block.scalar
        def _(s):
            s.wait_ge(sems["LDX"], M["CONVLD"])
            inc(s.activation(logdA_v, Alog_v, AF.Exp, scale=-1.0), "S")
        M["EXPA"] = cur("S")

        @block.vector
        def _(v):
            v.wait_ge(sems["S"], M["EXPA"])
            for dc in range(NDC):
                sl = slice(dc * NST, (dc + 1) * NST)
                inc(v.tensor_scalar(logdA_v[:, sl], logdA_v[:, sl],
                                    pv("dt", dc), -1.0, ALU.mult, ALU.mult), "V")
            v.drain()
            inc(v.tensor_tensor(Ct_v, Cm_v, Bm_v, ALU.mult), "V")
            v.drain()
            for dc in range(NDC):
                sl = slice(dc * NST, (dc + 1) * NST)
                inc(v.tensor_scalar(Ct_v[:, sl], Ct_v[:, sl],
                                    pv("dt", dc), None, ALU.mult, ALU.bypass), "V")
            v.drain()
        M["LOGDA_V"] = cur("V")

        @block.sync
        def _(sy):
            sy.wait_ge(sems["V"], M["LOGDA_V"])
            inc(sy.dma_start(out=t["logdA_dram"].rearrange("(dc p) n -> p dc n", p=128),
                             in_=logdA_v.rearrange("p (dc n) -> p dc n", n=NST)), "LDX", 16)
        M["LOGDA_D"] = cur("LDX")

        for dc in range(NDC):
            @block.vector
            def _(v, dc=dc):
                v.wait_ge(sems["V"], M["LOGDA_V"])
                if dc > 0:
                    v.wait_ge(sems["S"], cur("S"))
                ld = logdA_v[:, dc * NST:(dc + 1) * NST]
                in0 = ld.unsqueeze(1).broadcast_to([128, 64, NST])
                in1 = t["mramp"][:].unsqueeze(2).broadcast_to([128, 64, NST])
                inc(v.tensor_tensor(prodbuf.rearrange("p (m n) -> p m n", n=NST),
                                    in0, in1, ALU.mult), "V")

            @block.scalar
            def _(s, dc=dc):
                s.wait_ge(sems["V"], cur("V"))
                inc(s.activation(prodbuf, prodbuf, AF.Exp), "S")

            @block.vector
            def _(v, dc=dc):
                v.wait_ge(sems["S"], cur("S"))
                ct = Ct_v[:, dc * NST:(dc + 1) * NST]
                pb3 = prodbuf.rearrange("p (m n) -> p m n", n=NST)
                inc(v.tensor_tensor(pb3, pb3, ct.unsqueeze(1).broadcast_to([128, 64, NST]),
                                    ALU.mult), "V")
                v.drain()
                inc(v.tensor_reduce(t["k_loc"][:, dc * Q:(dc + 1) * Q], pb3, AX.X, ALU.add), "V")
                v.drain()
        M["KLOC_V"] = cur("V")

        @block.vector
        def _(v):
            kl0 = t["k_loc"][:].rearrange("p (dc m) -> p dc m", m=Q)[:, :, 0]
            inc(v.tensor_tensor(t["pvec"][:, 32:40], t["pvec"][:, 24:32], kl0, ALU.add), "V")
            v.drain()

        cp_evac = {}
        for dc in range(NDC):
            @block.tensor
            def _(tt, dc=dc):
                if dc == 0:
                    tt.wait_ge(sems["V"], M["KLOC_V"])
                    tt.wait_ge(sems["LDX"], M["LDX_SMALL"])
                else:
                    tt.wait_ge(sems["V"], cp_evac[dc - 1])
                inc(tt.transpose(ps_win[0][0:64, 0:128], logdA_v[:, dc * NST:(dc + 1) * NST],
                                 t["ident_f"][:]), "T")
                inc(tt.transpose(ps_win[1][0:64, 0:128], Ct_v[:, dc * NST:(dc + 1) * NST],
                                 t["ident_f"][:]), "T")

            @block.vector
            def _(v, dc=dc):
                v.wait_ge(sems["T"], cur("T"))
                for (dst, bank) in ((t["logdA_P2"], 0), (t["Ct_P2"], 1)):
                    src = ps_win[bank][0:64, 0:128].rearrange("n (dd two) -> n dd two", two=2)
                    inc(v.tensor_copy(dst[0:64, dc * 64:(dc + 1) * 64], src[:, :, 0]), "V")
                    inc(v.tensor_copy(dst[64:128, dc * 64:(dc + 1) * 64], src[:, :, 1]), "V")
                cp_evac[dc] = cur("V")
        M["P2_V"] = cur("V")

        @block.scalar
        def _(s):
            s.wait_ge(sems["V"], M["P2_V"])
            inc(s.activation(t["r64_P2"][:], t["logdA_P2"][:], AF.Exp, scale=64.0), "S")
        M["R64_S"] = cur("S")

    if stop >= PHASES.index("bloc"):
        ps_pb_t = es.enter_context(nc.psum_tensor("ps_pb", [128, 1024], F32))
        ps_pbe = ps_pb_t[:, 0:512]
        ps_pbo = ps_pb_t[:, 512:1024]
        ps_ys_t = es.enter_context(nc.psum_tensor("ps_ys", [128, 1024], F32))
        ps_ys = [ps_ys_t[:, 0:512], ps_ys_t[:, 512:1024]]

        pb_done = {}
        pb_used = {}
        ctp_used = {}
        ys_half = {}

        for g in range(8):
            for hh in range(2):
                @block.gpsimd
                def _(gp, g=g, hh=hh):
                    gp.wait_ge(sems["LDX"], M["LOGDA_D"])
                    if g == 0 and hh == 0:
                        gp.wait_ge(sems["V"], M["KLOC_V"])
                    elif hh == 0:
                        gp.wait_ge(sems["V"], M[("CTP_V", g - 1)])
                    else:
                        gp.wait_ge(sems["S"], cur("S"))
                    src = t["logdA_dram"].rearrange("d n -> (d n)")[
                        (g * 128 + 64 * hh) * 64:(g * 128 + 64 * hh + 64) * 64]
                    src = src.unsqueeze(0).broadcast_to([128, 4096])
                    inc(gp.dma_start(out=prodbuf, in_=src), "G", 16)

                @block.vector
                def _(v, g=g, hh=hh):
                    v.wait_ge(sems["G"], cur("G"))
                    inc(v.tensor_scalar(prodbuf, prodbuf, JNEG, None, ALU.mult, ALU.bypass), "V")
                M[("PBJ", g, hh)] = cur("V")

                @block.scalar
                def _(s, g=g, hh=hh):
                    s.wait_ge(sems["V"], M[("PBJ", g, hh)])
                    if g >= 1 and hh == 0:
                        s.wait_ge(sems["T"], pb_used[g - 1])
                    inc(s.activation(Pb_g[:, 4096 * hh:4096 * hh + 4096], prodbuf, AF.Exp), "S")
            M[("PB_S", g)] = cur("S")

            @block.vector
            def _(v, g=g):
                v.wait_ge(sems["S"], M[("PB_S", g)])
                v.wait_ge(sems["S"], M["R64_S"])
                if g >= 1:
                    v.wait_ge(sems["T"], ctp_used[g - 1])
                ld = t["logdA_P2"][:, g * 64:(g + 1) * 64]
                in0 = ld.unsqueeze(2).broadcast_to([128, 64, 64])
                in1 = t["iramp1"][:].unsqueeze(1).broadcast_to([128, 64, 64])
                inc(v.tensor_tensor(prodbuf.rearrange("p (pr i) -> p pr i", i=64),
                                    in0, in1, ALU.mult), "V")

            @block.scalar
            def _(s, g=g):
                s.wait_ge(sems["V"], cur("V"))
                inc(s.activation(prodbuf, prodbuf, AF.Exp), "S")

            @block.vector
            def _(v, g=g):
                v.wait_ge(sems["S"], cur("S"))
                ct = t["Ct_P2"][:, g * 64:(g + 1) * 64]
                inc(v.tensor_tensor(CtP_g.rearrange("p (pr i) -> p pr i", i=64),
                                    prodbuf.rearrange("p (pr i) -> p pr i", i=64),
                                    ct.unsqueeze(2).broadcast_to([128, 64, 64]), ALU.mult), "V")
            M[("CTP_V", g)] = cur("V")

            for qq in range(4):
                @block.tensor
                def _(tt, g=g, qq=qq):
                    tt.wait_ge(sems["S"], M[("PB_S", g)])
                    tt.wait_ge(sems["V"], M["XPT_MASKED"])
                    if (g, qq) != (0, 0):
                        prev = (g, qq - 1) if qq > 0 else (g - 1, 3)
                        tt.wait_ge(sems["V"], pb_done[prev])
                    xv = xpT.rearrange("p (c d) -> p c d", d=D)
                    for pr in range(16):
                        for dpar in range(2):
                            din = qq * 32 + pr * 2 + dpar
                            d = g * 128 + din
                            inc(tt.matmul(ps_pbe[64 * dpar:64 * dpar + 64, pr * 32:(pr + 1) * 32],
                                          Pb_g[0:64, din * 64:(din + 1) * 64],
                                          xv[0:64, :, d], start=True, stop=True,
                                          tile_position=(0, 64 * dpar)), "T")
                            inc(tt.matmul(ps_pbo[64 * dpar:64 * dpar + 64, pr * 32:(pr + 1) * 32],
                                          Pb_g[64:128, din * 64:(din + 1) * 64],
                                          xv[64:128, :, d], start=True, stop=True,
                                          tile_position=(64, 64 * dpar)), "T")
                    if qq == 3:
                        pb_used[g] = cur("T")

                @block.vector
                def _(v, g=g, qq=qq):
                    v.wait_ge(sems["T"], cur("T"))
                    b3 = B_sb.rearrange("p (pr c two) -> p pr c two", c=32, two=2)
                    inc(v.tensor_copy(b3[:, :, :, 0],
                                      ps_pbe.rearrange("p (pr c) -> p pr c", c=32)), "V")
                    inc(v.tensor_copy(b3[:, :, :, 1],
                                      ps_pbo.rearrange("p (pr c) -> p pr c", c=32)), "V")
                    pb_done[(g, qq)] = cur("V")
                    v.drain()
                    for pr in range(16):
                        prg = qq * 16 + pr
                        inc(v.tensor_tensor_scan(
                            S_g[:, prg * 64:(prg + 1) * 64],
                            t["r64_P2"][:, g * 64 + prg:g * 64 + prg + 1].broadcast_to([128, 64]),
                            B_sb[:, pr * 64:(pr + 1) * 64],
                            0.0, ALU.mult, ALU.add), "V")
            M[("SCAN_V", g)] = cur("V")

            for half in range(2):
                @block.tensor
                def _(tt, g=g, half=half):
                    tt.wait_ge(sems["V"], M[("SCAN_V", g)])
                    tt.wait_ge(sems["V"], M[("CTP_V", g)])
                    if (g, half) != (0, 0):
                        prev = (g, 0) if half == 1 else (g - 1, 1)
                        tt.wait_ge(sems["V"], ys_half[prev])
                    for prh in range(32):
                        pr = half * 32 + prh
                        for dpar in range(2):
                            lhsT = CtP_g[64 * dpar:64 * dpar + 64, pr * 64:(pr + 1) * 64]
                            Sr = S_g[64 * dpar:64 * dpar + 64, pr * 64:(pr + 1) * 64]
                            mve = Sr[:, 31:63].rearrange("n (c two) -> n c two", two=2)[:, :, 0]
                            mvo = Sr[:, 32:64].rearrange("n (c two) -> n c two", two=2)[:, :, 0]
                            inc(tt.matmul(ps_ys[dpar][0:64, prh * 16:(prh + 1) * 16],
                                          lhsT, mve, start=True, stop=True,
                                          tile_position=(64 * dpar, 0)), "T")
                            inc(tt.matmul(ps_win[dpar][64:128, prh * 16:(prh + 1) * 16],
                                          lhsT, mvo, start=True, stop=True,
                                          tile_position=(64 * dpar, 64)), "T")
                    if half == 1:
                        ctp_used[g] = cur("T")

                @block.vector
                def _(v, g=g, half=half):
                    v.wait_ge(sems["T"], cur("T"))
                    for dpar in range(2):
                        d0 = g * 128 + 64 * half + dpar
                        dhi0 = d0 // 2
                        yte = y_T[0:64, :].rearrange("r (c dhi two) -> r dhi two c",
                                                     two=2, dhi=512)[:, dhi0:dhi0 + 32, dpar, :]
                        srce = ps_ys[dpar][0:64, :].rearrange("r (s c) -> r s c", c=16)
                        inc(v.tensor_copy(yte, srce), "V")
                        yto = y_T[64:128, :].rearrange("r (c dhi two) -> r dhi two c",
                                                       two=2, dhi=512)[:, dhi0:dhi0 + 32, dpar, :]
                        srco = ps_win[dpar][64:128, :].rearrange("r (s c) -> r s c", c=16)
                        inc(v.tensor_copy(yto, srco), "V")
                    ys_half[(g, half)] = cur("V")
        M["YT_DONE"] = cur("V")
        M["BLOC_T"] = cur("T")

    if stop >= PHASES.index("local"):
        xb_evac = {}
        for c in range(16):
            for db in range(NDC):
                k = c * NDC + db
                bank = k % 2

                @block.tensor
                def _(tt, c=c, db=db, k=k, bank=bank):
                    tt.wait_ge(sems["T"], M["BLOC_T"])
                    tt.wait_ge(sems["V"], M["YT_DONE"])
                    if k >= 2:
                        tt.wait_ge(sems["V"], xb_evac[k - 2])
                    inc(tt.transpose(ps_tr[bank][:, 0:128],
                                     xpT[:, (16 + c) * 1024 + db * 128:(16 + c) * 1024 + (db + 1) * 128],
                                     t["ident_bf"][:]), "T")

                @block.vector
                def _(v, c=c, db=db, k=k, bank=bank):
                    v.wait_ge(sems["T"], cur("T"))
                    inc(v.tensor_copy(xpdt[:, db * 2048 + c * 128: db * 2048 + (c + 1) * 128],
                                      ps_tr[bank][:, 0:128]), "V")
                    xb_evac[k] = cur("V")
        M["XPDT_V"] = cur("V")

        ytb_evac = {}
        ycast = {}
        for dc in range(NDC):
            @block.vector
            def _(v, dc=dc):
                v.wait_ge(sems["V"], M["XPDT_V"])
                xsl = xpdt[:, dc * 2048:(dc + 1) * 2048]
                inc(v.tensor_scalar(y_dc, xsl, pv("dpk0", dc), None, ALU.mult, ALU.bypass), "V")
                v.drain()
                y4 = y_dc.rearrange("p (c i) -> p c i", i=64)
                x4 = xsl.rearrange("p (c i) -> p c i", i=64)
                for m in range(1, 64):
                    inc(v.scalar_tensor_tensor(y4[:, :, m:64], x4[:, :, 0:64 - m],
                                               t["k_loc"][:, dc * Q + m:dc * Q + m + 1],
                                               y4[:, :, m:64], ALU.mult, ALU.add), "V")
                    v.drain()
            M[("TAPS", dc)] = cur("V")

            for c in range(16):
                bank = c % 2

                @block.tensor
                def _(tt, dc=dc, c=c, bank=bank):
                    tt.wait_ge(sems["V"], M[("TAPS", dc)])
                    key = (dc, c - 2) if c >= 2 else ((dc - 1, c + 14) if dc > 0 else None)
                    if key is not None:
                        tt.wait_ge(sems["V"], ytb_evac[key])
                    inc(tt.transpose(ps_tr[bank][:, 128:256],
                                     y_T[:, c * 1024 + dc * 128: c * 1024 + (dc + 1) * 128],
                                     t["ident_bf"][:]), "T")

                @block.vector
                def _(v, dc=dc, c=c, bank=bank):
                    v.wait_ge(sems["T"], cur("T"))
                    dst = y_dc[:, c * 128:(c + 1) * 128]
                    inc(v.tensor_tensor(dst, dst, ps_tr[bank][:, 128:256], ALU.add), "V")
                    ytb_evac[(dc, c)] = cur("V")

            @block.vector
            def _(v, dc=dc):
                v.drain()
                xsl = xpdt[:, dc * 2048:(dc + 1) * 2048]
                inc(v.tensor_copy(xsl, y_dc), "V")
                v.drain()
            ycast[dc] = cur("V")
        M["Y_V"] = cur("V")

    if stop >= PHASES.index("router"):
        @block.sync
        def _(sy):
            sy.wait_ge(sems["T"], cur("T"))
            sy.wait_ge(sems["V"], M["Y_V"])
            inc(sy.dma_start(out=acc.rearrange("p (dc k) -> p dc k", k=TOWN),
                             in_=t["xoT"].rearrange("(dc p) k -> p dc k", p=128)), "LDX", 16)
        M["ACC_LD"] = cur("LDX")

        @block.tensor
        def _(tt):
            tt.wait_ge(sems["LDX"], M["ACC_LD"])
            for dc in range(NDC):
                inc(tt.matmul(ps_win[0][0:8, 0:1], t["WrT_sb"][:, dc * E:(dc + 1) * E],
                              ONES, start=(dc == 0), stop=(dc == NDC - 1)), "T")

        @block.vector
        def _(v):
            v.wait_ge(sems["T"], cur("T"))
            inc(v.tensor_copy(t["wsum_sb"][:], ps_win[0][0:8, 0:1]), "V")
            v.drain()
            inc(v.tensor_scalar(t["wsum_diag"][:], t["ident8"][:], t["wsum_sb"][:, 0:1], None,
                                ALU.mult, ALU.bypass), "V")
        M["WSUMD"] = cur("V")

        @block.tensor
        def _(tt):
            tt.wait_ge(sems["V"], M["WSUMD"])
            inc(tt.matmul(ps_win[0][:, 8:16], t["ones8"][:], t["wsum_diag"][:],
                          start=True, stop=True), "T")

        @block.vector
        def _(v):
            v.wait_ge(sems["T"], cur("T"))
            inc(v.tensor_copy(t["wsum_bc"][:], ps_win[0][:, 8:16]), "V")
        M["WSUM_BC"] = cur("V")

        lr_evac = {}
        for tt4 in range(4):
            @block.tensor
            def _(tt, tt4=tt4):
                if tt4 >= 1:
                    tt.wait_ge(sems["V"], lr_evac[tt4 - 1])
                else:
                    tt.wait_ge(sems["V"], M["WSUM_BC"])
                for dc in range(NDC):
                    mv = acc[:, dc * 2048 + tt4 * 512: dc * 2048 + tt4 * 512 + 512]
                    inc(tt.matmul(ps_win[1][0:8, 0:512], t["WrT_sb"][:, dc * E:(dc + 1) * E],
                                  mv, start=(dc == 0), stop=(dc == NDC - 1)), "T")

            @block.vector
            def _(v, tt4=tt4):
                v.wait_ge(sems["T"], cur("T"))
                inc(v.tensor_copy(lraw[:, tt4 * 512:(tt4 + 1) * 512], ps_win[1][0:8, 0:512]), "V")
                lr_evac[tt4] = cur("V")
        M["LRAW"] = cur("V")

        rt_evac = {}
        for tk in range(16):
            bank = tk % 2

            @block.tensor
            def _(tt, tk=tk, bank=bank):
                tt.wait_ge(sems["V"], M["LRAW"])
                if tk >= 2:
                    tt.wait_ge(sems["V"], rt_evac[tk - 2])
                inc(tt.transpose(ps_win[bank][0:128, 0:8],
                                 lraw[:, tk * 128:(tk + 1) * 128], t["ident8"][:]), "T")

            @block.vector
            def _(v, tk=tk, bank=bank):
                v.wait_ge(sems["T"], cur("T"))
                inc(v.tensor_copy(raw_tok[:, tk * 8:(tk + 1) * 8], ps_win[bank][0:128, 0:8]), "V")
                rt_evac[tk] = cur("V")

        r3 = lambda x: x.rearrange("p (k e) -> p k e", e=8)
        rmv = lambda a: rm_v[:, a * 16:(a + 1) * 16]
        bc8 = lambda x: x.unsqueeze(2).broadcast_to([128, 16, 8])

        @block.vector
        def _(v):
            v.wait_ge(sems["V"], cur("V"))
            rstd_own = t["rstd_all"][:, 16:32]
            mean_own = t["mvall"][:].rearrange("p (k two) -> p k two", two=2)[:, 16:32, 0]
            inc(v.tensor_tensor(rmv(0), mean_own, rstd_own, ALU.mult), "V")
            v.drain()
            inc(v.tensor_tensor(r3(nl_v), r3(raw_tok), bc8(rstd_own), ALU.mult), "V")
            v.drain()
            wb = t["wsum_bc"][:].unsqueeze(1).broadcast_to([128, 16, 8])
            inc(v.tensor_tensor(r3(t2_v), wb, bc8(rmv(0)), ALU.mult), "V")
            v.drain()
            inc(v.tensor_tensor(r3(nl_v), r3(t2_v), r3(nl_v), ALU.subtract), "V")
            v.drain()
            inc(v.tensor_reduce(rmv(1), r3(nl_v), AX.X, ALU.min), "V")
            v.drain()
            inc(v.tensor_tensor(r3(ez_v), bc8(rmv(1)), r3(nl_v), ALU.subtract), "V")
        M["RZ"] = cur("V")

        @block.scalar
        def _(s):
            s.wait_ge(sems["V"], M["RZ"])
            inc(s.activation(ez_v, ez_v, AF.Exp), "S")
        M["REZ"] = cur("S")

        @block.vector
        def _(v):
            v.wait_ge(sems["S"], M["REZ"])
            inc(v.tensor_reduce(rmv(4), r3(ez_v), AX.X, ALU.add), "V")
            inc(v.tensor_reduce(rmv(2), r3(ez_v), AX.X, ALU.max), "V")
            v.drain()
            inc(v.tensor_tensor(r3(msk1_v), r3(ez_v), bc8(rmv(2)), ALU.subtract), "V")
            v.drain()
            inc(v.tensor_scalar(msk1_v, msk1_v, 0.0, None, ALU.is_ge, ALU.bypass), "V")
            v.drain()
            inc(v.scalar_tensor_tensor(s2_v, msk1_v, 2.0, ez_v, ALU.mult, ALU.subtract), "V")
            v.drain()
            inc(v.tensor_reduce(rmv(3), r3(s2_v), AX.X, ALU.min), "V")
            v.drain()
            inc(v.tensor_tensor(r3(msk2_v), r3(s2_v), bc8(rmv(3)), ALU.subtract), "V")
            v.drain()
            inc(v.tensor_scalar(msk2_v, msk2_v, 0.0, None, ALU.is_le, ALU.bypass), "V")
            v.drain()
            inc(v.scalar_tensor_tensor(rmv(5), rmv(4), EPS_TOP, rmv(2), ALU.mult, ALU.add), "V")
            v.drain()
            inc(v.tensor_tensor(rmv(5), rmv(5), rmv(3), ALU.subtract), "V")
            v.drain()
            inc(v.reciprocal(rmv(5), rmv(5)), "V")
            v.drain()
            inc(v.tensor_tensor(r3(comb_v), r3(msk1_v), bc8(rmv(2)), ALU.mult), "V")
            inc(v.tensor_tensor(r3(t2_v), r3(msk2_v), bc8(rmv(3)), ALU.mult), "V")
            v.drain()
            inc(v.tensor_tensor(comb_v, comb_v, t2_v, ALU.subtract), "V")
            v.drain()
            inc(v.tensor_tensor(r3(comb_v), r3(comb_v), bc8(rmv(5)), ALU.mult), "V")
        M["COMB"] = cur("V")

        ct_evac = {}
        for tk in range(16):
            bank = tk % 2

            @block.tensor
            def _(tt, tk=tk, bank=bank):
                tt.wait_ge(sems["V"], M["COMB"])
                if tk >= 2:
                    tt.wait_ge(sems["V"], ct_evac[tk - 2])
                inc(tt.transpose(ps_win[bank][0:8, 0:128],
                                 comb_v[:, tk * 8:(tk + 1) * 8], t["ident_f"][:]), "T")

            @block.vector
            def _(v, tk=tk, bank=bank):
                v.wait_ge(sems["T"], cur("T"))
                inc(v.tensor_copy(t["comb_T"][:, tk * 128:(tk + 1) * 128],
                                  ps_win[bank][0:8, 0:128]), "V")
                ct_evac[tk] = cur("V")
        M["COMB_T"] = cur("V")

    if stop >= PHASES.index("wout"):
        @block.gpsimd
        def _(g):
            g.wait_ge(sems["T"], M["BLOC_T"])
            g.wait_ge(sems["V"], M["Y_V"])
            inc(g.dma_start(out=Wout_bf.rearrange("p (dc o) -> p dc o", o=D),
                            in_=t["WoutT"].rearrange("(dc p) o -> p dc o", p=128)), "LDW", 16)
        M["WOUT_LD"] = cur("LDW")

        wo_evac = {}
        for tt4 in range(4):
            for doutc in range(NDC):
                k = tt4 * NDC + doutc
                bank = k % 2

                @block.tensor
                def _(tt, tt4=tt4, doutc=doutc, k=k, bank=bank):
                    tt.wait_ge(sems["LDW"], M["WOUT_LD"])
                    tt.wait_ge(sems["V"], M["Y_V"])
                    tt.wait_ge(sems["LDX"], M["ACC_LD"])
                    if k >= 2:
                        tt.wait_ge(sems["V"], wo_evac[k - 2])
                    for dc in range(NDC):
                        mv = xpdt[:, dc * 2048 + tt4 * 512: dc * 2048 + tt4 * 512 + 512]
                        inc(tt.matmul(ps_win[bank],
                                      Wout_bf[:, dc * D + doutc * 128: dc * D + (doutc + 1) * 128],
                                      mv, start=(dc == 0), stop=(dc == NDC - 1)), "T")

                @block.vector
                def _(v, tt4=tt4, doutc=doutc, k=k, bank=bank):
                    v.wait_ge(sems["T"], cur("T"))
                    a = acc[:, doutc * 2048 + tt4 * 512: doutc * 2048 + tt4 * 512 + 512]
                    inc(v.scalar_tensor_tensor(a, ps_win[bank], pv("bout", doutc),
                                               a, ALU.add, ALU.add), "V")
                    wo_evac[k] = cur("V")
        M["WOUT_V"] = cur("V")
        M["WOUT_T"] = cur("T")

    if stop >= PHASES.index("moe"):
        @block.gpsimd
        def _(gp):
            gp.wait_ge(sems["V"], M["COMB_T"])
            gp.wait_ge(sems["T"], M["WOUT_T"])
            inc(gp.dma_start(out=sel_bf, in_=t["sel_c"]), "LDW", 16)
        M["SEL_LD"] = cur("LDW")

        wg_used = {}
        wd_used = {}
        h_done = {}
        gr_done = {}
        cb_evac = {}
        mo_evac = {}

        for e in range(E):
            for tt4 in range(4):
                @block.tensor
                def _(tt, e=e, tt4=tt4):
                    tt.wait_ge(sems["LDW"], M["SEL_LD"])
                    if (e, tt4) != (0, 0):
                        prev = (e, tt4 - 1) if tt4 > 0 else (e - 1, 3)
                        tt.wait_ge(sems["V"], cb_evac[prev])
                    if e > 0:
                        tt.wait_ge(sems["V"], M[("A_DONE", e - 1, 3)])
                    inc(tt.matmul(ps_pbo[:, 0:512], sel_bf[:, e * 128:(e + 1) * 128],
                                  t["comb_T"][:, tt4 * 512:(tt4 + 1) * 512],
                                  start=True, stop=True), "T")

                @block.vector
                def _(v, e=e, tt4=tt4):
                    v.wait_ge(sems["T"], cur("T"))
                    inc(v.tensor_copy(comb_bc[:, tt4 * 512:(tt4 + 1) * 512], ps_pbo[:, 0:512]), "V")
                    cb_evac[(e, tt4)] = cur("V")
            M[("CB", e)] = cur("V")

            for qtr in range(4):
                for sub in range(2):
                    ei = qtr * 2 + sub
                    slot = ei % 2

                    @block.gpsimd
                    def _(gp, e=e, ei=ei, slot=slot):
                        if e == 0 and ei < 2:
                            gp.wait_ge(sems["T"], M["WOUT_T"])
                            gp.wait_ge(sems["V"], M["WOUT_V"])
                        key = (e, ei - 2) if ei >= 2 else ((e - 1, 6 + ei) if e > 0 else None)
                        if key is not None:
                            gp.wait_ge(sems["T"], wg_used[key])
                        src = t["WgT"][e, :, ei * 512:(ei + 1) * 512]
                        inc(gp.dma_start(out=Wg_bf(slot).rearrange("p (dc f) -> p dc f", f=512),
                                         in_=src.rearrange("(dc p) f -> p dc f", p=128)), "LDW", 16)
                        src2 = t["WuT"][e, :, ei * 512:(ei + 1) * 512]
                        inc(gp.dma_start(out=Wu_bf(slot).rearrange("p (dc f) -> p dc f", f=512),
                                         in_=src2.rearrange("(dc p) f -> p dc f", p=128)), "LDW", 16)
                    M[("WGDMA", e, ei)] = cur("LDW")

                @block.gpsimd
                def _(gp, e=e, qtr=qtr):
                    if e == 0 and qtr == 0:
                        gp.wait_ge(sems["T"], M["WOUT_T"])
                        gp.wait_ge(sems["V"], M["WOUT_V"])
                    key = (e, qtr - 1) if qtr >= 1 else ((e - 1, 3) if e > 0 else None)
                    if key is not None:
                        gp.wait_ge(sems["T"], wd_used[key])
                    src = t["WdT"][e, qtr * 1024:(qtr + 1) * 1024, :]
                    inc(gp.dma_start(out=Wd_bf.rearrange("p (q o) -> p q o", o=D),
                                     in_=src.rearrange("(q p) o -> p q o", p=128)), "LDW", 16)
                M[("WDDMA", e, qtr)] = cur("LDW")

                for sub in range(2):
                    ei = qtr * 2 + sub
                    slot = ei % 2
                    for tt4 in range(4):
                        for dffc in range(4):
                            kk = tt4 * 4 + dffc
                            dffq = sub * 4 + dffc

                            @block.tensor
                            def _(tt, e=e, ei=ei, slot=slot, tt4=tt4, dffc=dffc, kk=kk):
                                tt.wait_ge(sems["LDW"], M[("WGDMA", e, ei)])
                                if (e, ei, kk) == (0, 0, 0):
                                    tt.wait_ge(sems["V"], M[("CB", 0)])
                                key = (e, ei, kk - 1)
                                if key in h_done:
                                    tt.wait_ge(sems["V"], h_done[key])
                                else:
                                    pk = (e, ei - 1, 15) if ei > 0 else ((e - 1, 7, 15) if e > 0 else None)
                                    if pk is not None:
                                        tt.wait_ge(sems["V"], h_done[pk])
                                for dc in range(NDC):
                                    mv = t["xn_T"][:, dc * 2048 + tt4 * 512: dc * 2048 + tt4 * 512 + 512]
                                    inc(tt.matmul(ps_win[0],
                                                  Wg_bf(slot)[:, dc * 512 + dffc * 128: dc * 512 + (dffc + 1) * 128],
                                                  mv, start=(dc == 0), stop=(dc == NDC - 1)), "T")
                                for dc in range(NDC):
                                    mv = t["xn_T"][:, dc * 2048 + tt4 * 512: dc * 2048 + tt4 * 512 + 512]
                                    inc(tt.matmul(ps_win[1],
                                                  Wu_bf(slot)[:, dc * 512 + dffc * 128: dc * 512 + (dffc + 1) * 128],
                                                  mv, start=(dc == 0), stop=(dc == NDC - 1)), "T")
                                if kk == 15:
                                    wg_used[(e, ei)] = cur("T")

                            @block.scalar
                            def _(s, e=e, ei=ei, kk=kk):
                                s.wait_ge(sems["T"], cur("T") - 8)
                                key = (e, ei, kk - 2)
                                if key in h_done:
                                    s.wait_ge(sems["V"], h_done[key])
                                inc(s.activation(gr_sb(kk % 2), ps_win[0], AF.Relu), "S")
                                gr_done[(e, ei, kk)] = cur("S")

                            @block.vector
                            def _(v, e=e, ei=ei, tt4=tt4, kk=kk, dffq=dffq):
                                v.wait_ge(sems["T"], cur("T"))
                                v.wait_ge(sems["S"], gr_done[(e, ei, kk)])
                                hsl = h_all[:, (dffq * 4 + tt4) * 512:(dffq * 4 + tt4) * 512 + 512]
                                inc(v.tensor_tensor(hsl, gr_sb(kk % 2), ps_win[1], ALU.mult), "V")
                                v.drain()
                                inc(v.tensor_tensor(hsl, hsl,
                                                    comb_bc[:, tt4 * 512:(tt4 + 1) * 512],
                                                    ALU.mult), "V")
                                h_done[(e, ei, kk)] = cur("V")
                M[("A_DONE", e, qtr)] = cur("V")

                for tt4 in range(4):
                    for doutc in range(NDC):
                        k = tt4 * NDC + doutc
                        bank = k % 2

                        @block.tensor
                        def _(tt, e=e, qtr=qtr, tt4=tt4, doutc=doutc, k=k, bank=bank):
                            tt.wait_ge(sems["LDW"], M[("WDDMA", e, qtr)])
                            tt.wait_ge(sems["V"], M[("A_DONE", e, qtr)])
                            key = (e, qtr, k - 2)
                            if key in mo_evac:
                                tt.wait_ge(sems["V"], mo_evac[key])
                            else:
                                pk = (e, qtr - 1, 30 + k) if qtr > 0 else \
                                     ((e - 1, 3, 30 + k) if e > 0 else None)
                                if pk is not None:
                                    tt.wait_ge(sems["V"], mo_evac[pk])
                            for dffq in range(8):
                                mv = h_all[:, (dffq * 4 + tt4) * 512:(dffq * 4 + tt4) * 512 + 512]
                                inc(tt.matmul(ps_ys[bank],
                                              Wd_bf[:, dffq * D + doutc * 128: dffq * D + (doutc + 1) * 128],
                                              mv, start=(dffq == 0), stop=(dffq == 7)), "T")
                            if k == 31:
                                wd_used[(e, qtr)] = cur("T")

                        @block.vector
                        def _(v, e=e, qtr=qtr, tt4=tt4, doutc=doutc, k=k, bank=bank):
                            v.wait_ge(sems["T"], cur("T"))
                            a = acc[:, doutc * 2048 + tt4 * 512: doutc * 2048 + tt4 * 512 + 512]
                            inc(v.tensor_tensor(a, a, ps_ys[bank], ALU.add), "V")
                            mo_evac[(e, qtr, k)] = cur("V")
        M["MOE_V"] = cur("V")

    if stop >= PHASES.index("store"):
        st_evac = {}
        for tk in range(16):
            for dc in range(NDC):
                k = tk * NDC + dc
                bank = k % 2

                @block.tensor
                def _(tt, tk=tk, dc=dc, k=k, bank=bank):
                    tt.wait_ge(sems["V"], M.get("MOE_V", M.get("WOUT_V", cur("V"))))
                    if k >= 2:
                        tt.wait_ge(sems["V"], st_evac[k - 2])
                    inc(tt.transpose(ps_win[bank][:, 0:128],
                                     acc[:, dc * 2048 + tk * 128: dc * 2048 + (tk + 1) * 128],
                                     t["ident_f"][:]), "T")

                @block.vector
                def _(v, tk=tk, dc=dc, k=k, bank=bank):
                    v.wait_ge(sems["T"], cur("T"))
                    if dc == 0 and tk >= 1:
                        v.wait_ge(sems["ST"], 16 * tk)
                    inc(v.tensor_copy(out_stg[:, dc * 128:(dc + 1) * 128],
                                      ps_win[bank][:, 0:128]), "V")
                    st_evac[k] = cur("V")

            @block.sync
            def _(sy, tk=tk):
                sy.wait_ge(sems["V"], st_evac[tk * NDC + NDC - 1])
                inc(sy.dma_start(out=t["out_tok"][tk * 128:(tk + 1) * 128, :], in_=out_stg), "ST", 16)

        @block.sync
        def _(sy):
            sy.wait_ge(sems["ST"], cur("ST"))

    dbg_views = {
        "xn_T": t["xn_T"][:], "xpT": xpT, "xpdt": xpdt, "mvall": t["mvall"][:],
        "rstd_all": t["rstd_all"][:], "k_loc": t["k_loc"][:], "logdA": logdA_v,
        "Ct": Ct_v, "logdA_P2": t["logdA_P2"][:], "Ct_P2": t["Ct_P2"][:],
        "r64_P2": t["r64_P2"][:], "Pb_g": Pb_g, "CtP_g": CtP_g, "S_g": S_g,
        "y_T": y_T, "B_sb": B_sb, "acc": acc, "comb_T": t["comb_T"][:],
        "comb": comb_v, "ez": ez_v, "raw_tok": raw_tok, "lraw": lraw,
        "wsum_bc": t["wsum_bc"][:], "h_all": h_all, "comb_bc": comb_bc,
        "y_dc": y_dc, "dpk0": t["pvec"][:, 32:40], "xn_tok": xn_tok,
    }
    if debug_keys:
        dbg_out = {}
        for name in debug_keys:
            ap = dbg_views[name]
            dbg_out[name] = nc.dram_tensor("dbg_" + name, list(ap.shape), ap.dtype,
                                           kind="ExternalOutput").ap()

        @block.sync
        def _(sy):
            for s in ["V", "S", "T", "G"]:
                sy.wait_ge(sems[s], cur(s))
            for name in debug_keys:
                inc(sy.dma_start(out=dbg_out[name], in_=dbg_views[name]), "ST", 16)
            sy.wait_ge(sems["ST"], cur("ST"))

    es.close()
    return nc



import ml_dtypes
BF = ml_dtypes.bfloat16



def reshape_dvec(v):
    return np.ascontiguousarray(v.reshape(NDC, 128).T)


def reshape_dmat(m):
    K = m.shape[1]
    return np.ascontiguousarray(m.reshape(NDC, 128, K).transpose(1, 0, 2).reshape(128, NDC * K))


def consts():
    c = {}
    c["ident_c"] = np.eye(128, dtype=np.float32)
    sel = np.zeros((8, E * 128), dtype=np.float32)
    for e in range(E):
        sel[e, e * 128:(e + 1) * 128] = 1.0
    c["sel_c"] = sel
    c["ones8_c"] = np.ones((8, 128), dtype=np.float32)
    c["ident8_c"] = np.eye(8, dtype=np.float32)
    smalls = np.zeros((128, 16), dtype=np.float32)
    smalls[:, 0] = 1.0
    smalls[:, 1] = 63.0 - (np.arange(128) % 64)
    c["smalls_c"] = smalls
    c["iramp1_c"] = np.tile(np.arange(1, 65, dtype=np.float32), (128, 1))
    c["mramp_c"] = np.tile(np.arange(64, dtype=np.float32), (128, 1))
    return c


def core_inputs(inputs, core):
    b, h = core // 2, core % 2
    x = np.asarray(inputs["L0_attn_x"])
    own = x[b, h * TOWN:(h + 1) * TOWN, :]
    prev = x[b, 0:TOWN, :] if h == 1 else np.zeros((TOWN, D), np.float32)
    m = {}
    m["xf"] = np.ascontiguousarray(np.concatenate([prev, own], axis=0))
    m["xoT"] = np.ascontiguousarray(own.T)
    m["cmask"] = np.full((128, 1), float(h), np.float32)
    m["WinT"] = np.ascontiguousarray(np.asarray(inputs["L0_attn_Win"]).T)
    m["binb"] = reshape_dvec(np.asarray(inputs["L0_attn_bin_"]))
    m["WoutT"] = np.ascontiguousarray(np.asarray(inputs["L0_attn_Wout"]).T)
    m["boutb"] = reshape_dvec(np.asarray(inputs["L0_attn_bout"]))
    m["A_log_r"] = reshape_dmat(np.asarray(inputs["L0_attn_A_log"]))
    m["B_r"] = reshape_dmat(np.asarray(inputs["L0_attn_B"]))
    m["C_r"] = reshape_dmat(np.asarray(inputs["L0_attn_C"]))
    m["dt_r"] = reshape_dvec(np.asarray(inputs["L0_attn_dt"]))
    m["Dp_r"] = reshape_dvec(np.asarray(inputs["L0_attn_Dp"]))
    m["WrT_r"] = reshape_dmat(np.ascontiguousarray(np.asarray(inputs["L0_ffn_Wr"]).T))
    m["WgT"] = np.ascontiguousarray(np.asarray(inputs["L0_ffn_Wg"]).transpose(0, 2, 1))
    m["WuT"] = np.ascontiguousarray(np.asarray(inputs["L0_ffn_Wu"]).transpose(0, 2, 1))
    m["WdT"] = np.ascontiguousarray(np.asarray(inputs["L0_ffn_Wd"]).transpose(0, 2, 1))
    m.update(consts())
    return m




_CACHE = {}


def kernel(**inputs):
    import numpy as np
    from concourse.bass_utils import run_bass_kernel_spmd
    if "nc" not in _CACHE:
        _CACHE["nc"] = build()
    nc = _CACHE["nc"]
    cores = list(range(8))
    maps = [core_inputs(inputs, c) for c in cores]
    res = run_bass_kernel_spmd(nc, maps, core_ids=cores)
    out = np.zeros((4, TFULL, D), np.float32)
    for ci, c in enumerate(cores):
        b, h = c // 2, c % 2
        out[b, h * TOWN:(h + 1) * TOWN, :] = np.asarray(res.results[ci]["out_tok"])
    return out


# version 03
# speedup vs baseline: 1.2435x; 1.2435x over previous
import contextlib
from collections import defaultdict

import numpy as np
import concourse.bass as bass
import concourse.mybir as mybir

F32 = mybir.dt.float32
BF16 = mybir.dt.bfloat16
AF = mybir.ActivationFunctionType
ALU = mybir.AluOpType
AX = mybir.AxisListType

D = 1024
NST = 64
TOWN = 2048
TFULL = 4096
Q = 64
NDC = 8
E = 8
DFF = 4096
EPS_LN = 1e-5
EPS_TOP = float(np.float32(1.1920929e-07))

PHASES = ["ln_win", "conv_params", "bloc", "local", "router", "wout", "moe", "store"]


def build(debug_keys=(), upto="store", ln_sub="full"):
    stop = PHASES.index(upto)
    nc = bass.Bass("TRN2", target_bir_lowering=False)
    es = contextlib.ExitStack()
    t = {}
    cnt = defaultdict(int)
    debug_keys = list(debug_keys)

    def dri(name, shape, dtype=F32):
        t[name] = nc.dram_tensor(name, list(shape), dtype, kind="ExternalInput").ap()

    def sbt(name, shape, dtype):
        t[name] = es.enter_context(nc.sbuf_tensor(name, list(shape), dtype))

    dri("xf", (TFULL, D))
    dri("xoT", (D, TOWN))
    dri("cmask", (128, 1))
    dri("WinT", (D, D))
    dri("binb", (128, NDC))
    dri("WoutT", (D, D))
    dri("boutb", (128, NDC))
    dri("A_log_r", (128, NDC * NST))
    dri("B_r", (128, NDC * NST))
    dri("C_r", (128, NDC * NST))
    dri("dt_r", (128, NDC))
    dri("Dp_r", (128, NDC))
    dri("WrT_r", (128, NDC * E))
    dri("WgT", (E, D, DFF))
    dri("WuT", (E, D, DFF))
    dri("WdT", (E, DFF, D))
    dri("ident_c", (128, 128))
    dri("sel_c", (8, E * 128))
    dri("ones8_c", (8, 128))
    dri("ident8_c", (8, 8))
    dri("smalls_c", (128, 16))
    dri("iramp1_c", (128, 64))
    dri("mramp_c", (128, 64))
    t["out_tok"] = nc.dram_tensor("out_tok", [TOWN, D], F32, kind="ExternalOutput").ap()
    t["logdA_dram"] = nc.dram_tensor("logdA_dram", [D, NST], F32, kind="Internal").ap()

    sbt("xn_T", (128, NDC * TOWN), BF16)
    sbt("ident_f", (128, 128), F32)
    sbt("ident_bf", (128, 128), BF16)
    sbt("ones8", (8, 128), F32)
    sbt("ident8", (8, 8), F32)
    sbt("smalls", (128, 16), F32)
    sbt("iramp1", (128, 64), F32)
    sbt("mramp", (128, 64), F32)
    sbt("pvec", (128, 64), F32)
    sbt("WrT_sb", (128, NDC * E), F32)
    sbt("mvall", (128, 64), F32)
    sbt("rstd_all", (128, 32), F32)
    sbt("stats6", (128, 12), F32)
    sbt("k_loc", (128, NDC * Q), F32)
    sbt("logdA_P2", (128, 512), F32)
    sbt("Ct_P2", (128, 512), F32)
    sbt("r64_P2", (128, 512), F32)
    sbt("comb_T", (8, TOWN), BF16)
    sbt("wsum_sb", (8, 1), F32)
    sbt("wsum_diag", (8, 8), F32)
    sbt("wsum_bc", (128, 8), F32)

    sbt("AA", (128, 32768), BF16)
    sbt("AB", (128, 32768), BF16)
    sbt("AC", (128, 16384), BF16)

    def av(arena, byte_off, ncols, dtype):
        assert byte_off % 4 == 0
        o = byte_off // 2
        if dtype == BF16:
            return t[arena][:, o:o + ncols]
        return t[arena][:, o:o + 2 * ncols].bitcast(F32)

    xf_roll = lambda slot: av("AA", 0, 4096, F32)
    xn_tok = av("AA", 16384, 4096, BF16)
    xn_roll = lambda slot: av("AA", 24576 + slot * 8192, 4096, BF16)
    xps = lambda slot: av("AA", 40960 + slot * 1024, 512, BF16)
    Win_bf = av("AA", 43008, 8192, BF16)
    Alog_v = av("AA", 0, 512, F32)
    Bm_v = av("AA", 2048, 512, F32)
    Cm_v = av("AA", 4096, 512, F32)
    logdA_v = av("AA", 6144, 512, F32)
    Ct_v = av("AA", 8192, 512, F32)
    prodbuf = av("AA", 10240, 4096, F32)
    Pb_g = av("AA", 26624, 8192, BF16)
    CtP_g = av("AA", 43008, 4096, BF16)
    B_sb = av("AA", 51200, 1024, F32)
    S_g = av("AA", 55296, 4096, BF16)
    xpdt = av("AA", 0, 16384, BF16)
    y_dc = av("AA", 32768, 2048, F32)
    Wout_bf = av("AA", 40960, 8192, BF16)
    h_all = av("AA", 0, 16384, BF16)
    Wg_bf = lambda slot: av("AA", 32768 + slot * 8192, 4096, BF16)
    Wu_bf = lambda slot: av("AA", 49152 + slot * 8192, 4096, BF16)

    xpT = av("AB", 0, 32768, BF16)
    acc = av("AB", 0, 16384, F32)

    y_T = av("AC", 0, 16384, BF16)
    lraw = av("AC", 0, 2048, F32)[0:8, :]
    raw_tok = av("AC", 8192, 128, F32)
    nl_v = av("AC", 8704, 128, F32)
    ez_v = av("AC", 9216, 128, F32)
    s2_v = av("AC", 9728, 128, F32)
    msk1_v = av("AC", 10240, 128, F32)
    msk2_v = av("AC", 10752, 128, F32)
    t2_v = av("AC", 11264, 128, F32)
    comb_v = av("AC", 11776, 128, F32)
    rm_v = av("AC", 12288, 96, F32)
    Wd_bf = av("AC", 0, 8192, BF16)
    comb_bc = av("AC", 16384, 2048, BF16)
    gr_sb = lambda slot: av("AC", 20480 + slot * 1024, 512, BF16)
    out_stg = av("AC", 22528, 1024, F32)
    sel_bf = av("AC", 26624, 1024, BF16)[0:8, :]

    sems = {}
    for s in ["LDX", "LDW", "T", "V", "S", "G", "ST"]:
        sems[s] = es.enter_context(nc.semaphore(s))
    block = es.enter_context(nc.Block())

    def inc(instr, s, n=1):
        cnt[s] += n
        return instr.then_inc(sems[s], n)

    def cur(s):
        return cnt[s]

    M = {}
    PV = {"bin": 0, "bout": 8, "dt": 16, "Dp": 24, "dpk0": 32, "cm": 40}

    def pv(name, j=0):
        o = PV[name] + j
        return t["pvec"][:, o:o + 1]

    ONES = t["smalls"][:, 0:1]
    JNEG = t["smalls"][:, 1:2]

    @block.sync
    def _(sy):
        for dst, src in [("ident_f", "ident_c"), ("ones8", "ones8_c"), ("ident8", "ident8_c"),
                         ("smalls", "smalls_c"), ("iramp1", "iramp1_c"), ("mramp", "mramp_c"),
                         ("WrT_sb", "WrT_r")]:
            inc(sy.dma_start(out=t[dst][:], in_=t[src]), "LDX", 16)
        inc(sy.dma_start(out=t["pvec"][:, 0:8], in_=t["binb"]), "LDX", 16)
        inc(sy.dma_start(out=t["pvec"][:, 8:16], in_=t["boutb"]), "LDX", 16)
        inc(sy.dma_start(out=t["pvec"][:, 16:24], in_=t["dt_r"]), "LDX", 16)
        inc(sy.dma_start(out=t["pvec"][:, 24:32], in_=t["Dp_r"]), "LDX", 16)
        inc(sy.dma_start(out=t["pvec"][:, 40:41], in_=t["cmask"]), "LDX", 16)
    M["LDX_SMALL"] = cur("LDX")

    @block.gpsimd
    def _(g):
        inc(g.dma_start(out=t["ident_bf"][:], in_=t["ident_c"]), "LDW", 16)
        inc(g.dma_start(out=Win_bf.rearrange("p (dc o) -> p dc o", o=D),
                        in_=t["WinT"].rearrange("(dc p) o -> p dc o", p=128)), "LDW", 16)
    M["LDW_INIT"] = cur("LDW")

    ps_win_t = es.enter_context(nc.psum_tensor("ps_win", [128, 1024], F32))
    ps_win = [ps_win_t[:, 0:512], ps_win_t[:, 512:1024]]
    ps_tr_t = es.enter_context(nc.psum_tensor("ps_tr", [128, 2048], BF16))
    ps_tr = [ps_tr_t[:, 0:1024], ps_tr_t[:, 1024:2048]]

    vslot_free = {}
    xn_evac = {}
    xps_evac = {}
    xpt_evac = {}

    for grp in range(8):
        own = grp >= 4
        slot = 0

        @block.scalar
        def _(s, grp=grp):
            if grp >= 1:
                s.wait_ge(sems["V"], vslot_free[grp - 1])
            src = t["xf"].rearrange("(g q p) d -> g p q d", g=8, p=128)[grp]
            inc(s.dma_start(out=xf_roll(0).rearrange("p (q d) -> p q d", d=D), in_=src),
                "LDX", 16)

        @block.vector
        def _(v, grp=grp):
            v.wait_ge(sems["LDX"], cur("LDX"))
            xr = xf_roll(0)
            for q in range(4):
                tile = 4 * grp + q
                inc(v.bn_stats(t["stats6"][:, 0:6], xr[:, q * 1024:q * 1024 + 512]), "V")
                inc(v.bn_stats(t["stats6"][:, 6:12], xr[:, q * 1024 + 512:(q + 1) * 1024]), "V")
                v.drain()
                inc(v.bn_aggr(t["mvall"][:, 2 * tile:2 * tile + 2], t["stats6"][:]), "V")
                v.drain()
            vs = t["mvall"][:].rearrange("p (k two) -> p k two", two=2)[:, 4 * grp:4 * grp + 4, 1]
            inc(v.tensor_scalar(t["rstd_all"][:, 4 * grp:4 * grp + 4], vs, EPS_LN, None,
                                ALU.add, ALU.bypass), "V")
            v.drain()
            inc(v.reciprocal(t["rstd_all"][:, 4 * grp:4 * grp + 4],
                             t["rstd_all"][:, 4 * grp:4 * grp + 4]), "V")

        @block.scalar
        def _(s, grp=grp):
            s.wait_ge(sems["V"], cur("V"))
            inc(s.sqrt(t["rstd_all"][:, 4 * grp:4 * grp + 4],
                       t["rstd_all"][:, 4 * grp:4 * grp + 4]), "S")

        @block.vector
        def _(v, grp=grp):
            v.wait_ge(sems["S"], cur("S"))
            if grp >= 1 and ("XNTR_T", grp - 1) in M:
                v.wait_ge(sems["T"], M[("XNTR_T", grp - 1)])
            xr = xf_roll(0)
            for q in range(4):
                tile = 4 * grp + q
                inc(v.tensor_scalar(xn_tok[:, q * 1024:(q + 1) * 1024],
                                    xr[:, q * 1024:(q + 1) * 1024],
                                    t["mvall"][:, 2 * tile:2 * tile + 1],
                                    t["rstd_all"][:, tile:tile + 1],
                                    ALU.subtract, ALU.mult), "V")
        vslot_free[grp] = cur("V")
        M[("XN_TOK", grp)] = cur("V")

        for dc in range(NDC if ln_sub != "stats" else 0):
            bank = dc % 2

            @block.tensor
            def _(tt, grp=grp, dc=dc, bank=bank):
                tt.wait_ge(sems["V"], M[("XN_TOK", grp)])
                tt.wait_ge(sems["LDW"], M["LDW_INIT"])
                key = (grp, dc - 2) if dc >= 2 else ((grp - 1, dc + 6) if grp > 0 else None)
                if key is not None:
                    tt.wait_ge(sems["V"], xn_evac[key])
                for q in range(4):
                    inc(tt.transpose(ps_tr[bank][:, q * 128:(q + 1) * 128],
                                     xn_tok[:, q * 1024 + dc * 128:q * 1024 + (dc + 1) * 128],
                                     t["ident_bf"][:]), "T")

            @block.vector
            def _(v, grp=grp, dc=dc, bank=bank, own=own):
                v.wait_ge(sems["T"], cur("T"))
                if own:
                    o0 = dc * 2048 + (grp - 4) * 512
                    dst = t["xn_T"][:, o0:o0 + 512]
                else:
                    if grp >= 2:
                        v.wait_ge(sems["T"], M[("WIN_T", grp - 2)])
                    dst = xn_roll(grp % 2)[:, dc * 512:(dc + 1) * 512]
                inc(v.tensor_copy(dst, ps_tr[bank][:, 0:512]), "V")
                xn_evac[(grp, dc)] = cur("V")
        M[("XN_RDY", grp)] = cur("V")
        M[("XNTR_T", grp)] = cur("T")

        for doutc in range(NDC if ln_sub == "full" else 0):
            bank = doutc % 2

            @block.tensor
            def _(tt, grp=grp, doutc=doutc, bank=bank, own=own):
                tt.wait_ge(sems["V"], M[("XN_RDY", grp)])
                key = (grp, doutc - 2) if doutc >= 2 else ((grp - 1, doutc + 6) if grp > 0 else None)
                if key is not None:
                    tt.wait_ge(sems["S"], xps_evac[key])
                for dc in range(NDC):
                    if own:
                        mv = t["xn_T"][:, dc * 2048 + (grp - 4) * 512: dc * 2048 + (grp - 4) * 512 + 512]
                    else:
                        mv = xn_roll(grp % 2)[:, dc * 512:(dc + 1) * 512]
                    inc(tt.matmul(ps_win[bank],
                                  Win_bf[:, dc * D + doutc * 128: dc * D + (doutc + 1) * 128],
                                  mv, start=(dc == 0), stop=(dc == NDC - 1)), "T")

            @block.scalar
            def _(s, grp=grp, doutc=doutc, bank=bank):
                s.wait_ge(sems["T"], cur("T"))
                key = (grp, doutc - 2) if doutc >= 2 else ((grp - 1, doutc + 6) if grp > 0 else None)
                if key is not None:
                    s.wait_ge(sems["V"], xpt_evac[key])
                inc(s.activation(xps(bank), ps_win[bank], AF.Identity,
                                 bias=pv("bin", doutc)), "S")
                xps_evac[(grp, doutc)] = cur("S")

            @block.tensor
            def _(tt, grp=grp, doutc=doutc, bank=bank):
                tt.wait_ge(sems["S"], cur("S"))
                for q in range(4):
                    inc(tt.transpose(ps_tr[bank][:, 512 + q * 128:512 + (q + 1) * 128],
                                     xps(bank)[:, q * 128:(q + 1) * 128],
                                     t["ident_bf"][:]), "T")

            @block.vector
            def _(v, grp=grp, doutc=doutc, bank=bank):
                v.wait_ge(sems["T"], cur("T"))
                dst = xpT.rearrange("p (c d) -> p c d", d=D)[
                    :, grp * 4:grp * 4 + 4, doutc * 128:(doutc + 1) * 128]
                src = ps_tr[bank][:, 512:1024].rearrange("p (q j) -> p q j", q=4)
                inc(v.tensor_copy(dst, src), "V")
                xpt_evac[(grp, doutc)] = cur("V")
        M[("WIN_T", grp)] = cur("T")

    @block.vector
    def _(v):
        v.wait_ge(sems["LDX"], M["LDX_SMALL"])
        ap = xpT[:, 0:16 * 1024]
        inc(v.tensor_scalar(ap, ap, pv("cm"), None, ALU.mult, ALU.bypass), "V")
    M["XPT_MASKED"] = cur("V")
    M["LN_T"] = cur("T")
    M["LN_V"] = cur("V")

    if stop >= PHASES.index("conv_params"):
        @block.sync
        def _(sy):
            sy.wait_ge(sems["T"], M["LN_T"])
            sy.wait_ge(sems["V"], M["LN_V"])
            inc(sy.dma_start(out=Alog_v, in_=t["A_log_r"]), "LDX", 16)
            inc(sy.dma_start(out=Bm_v, in_=t["B_r"]), "LDX", 16)
            inc(sy.dma_start(out=Cm_v, in_=t["C_r"]), "LDX", 16)
        M["CONVLD"] = cur("LDX")

        @block.scalar
        def _(s):
            s.wait_ge(sems["LDX"], M["CONVLD"])
            inc(s.activation(logdA_v, Alog_v, AF.Exp, scale=-1.0), "S")
        M["EXPA"] = cur("S")

        @block.vector
        def _(v):
            v.wait_ge(sems["S"], M["EXPA"])
            for dc in range(NDC):
                sl = slice(dc * NST, (dc + 1) * NST)
                inc(v.tensor_scalar(logdA_v[:, sl], logdA_v[:, sl],
                                    pv("dt", dc), -1.0, ALU.mult, ALU.mult), "V")
            v.drain()
            inc(v.tensor_tensor(Ct_v, Cm_v, Bm_v, ALU.mult), "V")
            v.drain()
            for dc in range(NDC):
                sl = slice(dc * NST, (dc + 1) * NST)
                inc(v.tensor_scalar(Ct_v[:, sl], Ct_v[:, sl],
                                    pv("dt", dc), None, ALU.mult, ALU.bypass), "V")
            v.drain()
        M["LOGDA_V"] = cur("V")

        @block.sync
        def _(sy):
            sy.wait_ge(sems["V"], M["LOGDA_V"])
            inc(sy.dma_start(out=t["logdA_dram"].rearrange("(dc p) n -> p dc n", p=128),
                             in_=logdA_v.rearrange("p (dc n) -> p dc n", n=NST)), "LDX", 16)
        M["LOGDA_D"] = cur("LDX")

        for dc in range(NDC):
            @block.vector
            def _(v, dc=dc):
                v.wait_ge(sems["V"], M["LOGDA_V"])
                if dc > 0:
                    v.wait_ge(sems["S"], cur("S"))
                ld = logdA_v[:, dc * NST:(dc + 1) * NST]
                in0 = ld.unsqueeze(1).broadcast_to([128, 64, NST])
                in1 = t["mramp"][:].unsqueeze(2).broadcast_to([128, 64, NST])
                inc(v.tensor_tensor(prodbuf.rearrange("p (m n) -> p m n", n=NST),
                                    in0, in1, ALU.mult), "V")

            @block.scalar
            def _(s, dc=dc):
                s.wait_ge(sems["V"], cur("V"))
                inc(s.activation(prodbuf, prodbuf, AF.Exp), "S")

            @block.vector
            def _(v, dc=dc):
                v.wait_ge(sems["S"], cur("S"))
                ct = Ct_v[:, dc * NST:(dc + 1) * NST]
                pb3 = prodbuf.rearrange("p (m n) -> p m n", n=NST)
                inc(v.tensor_tensor(pb3, pb3, ct.unsqueeze(1).broadcast_to([128, 64, NST]),
                                    ALU.mult), "V")
                v.drain()
                inc(v.tensor_reduce(t["k_loc"][:, dc * Q:(dc + 1) * Q], pb3, AX.X, ALU.add), "V")
                v.drain()
        M["KLOC_V"] = cur("V")

        @block.vector
        def _(v):
            kl0 = t["k_loc"][:].rearrange("p (dc m) -> p dc m", m=Q)[:, :, 0]
            inc(v.tensor_tensor(t["pvec"][:, 32:40], t["pvec"][:, 24:32], kl0, ALU.add), "V")
            v.drain()

        cp_evac = {}
        for dc in range(NDC):
            @block.tensor
            def _(tt, dc=dc):
                if dc == 0:
                    tt.wait_ge(sems["V"], M["KLOC_V"])
                    tt.wait_ge(sems["LDX"], M["LDX_SMALL"])
                else:
                    tt.wait_ge(sems["V"], cp_evac[dc - 1])
                inc(tt.transpose(ps_win[0][0:64, 0:128], logdA_v[:, dc * NST:(dc + 1) * NST],
                                 t["ident_f"][:]), "T")
                inc(tt.transpose(ps_win[1][0:64, 0:128], Ct_v[:, dc * NST:(dc + 1) * NST],
                                 t["ident_f"][:]), "T")

            @block.vector
            def _(v, dc=dc):
                v.wait_ge(sems["T"], cur("T"))
                for (dst, bank) in ((t["logdA_P2"], 0), (t["Ct_P2"], 1)):
                    src = ps_win[bank][0:64, 0:128].rearrange("n (dd two) -> n dd two", two=2)
                    inc(v.tensor_copy(dst[0:64, dc * 64:(dc + 1) * 64], src[:, :, 0]), "V")
                    inc(v.tensor_copy(dst[64:128, dc * 64:(dc + 1) * 64], src[:, :, 1]), "V")
                cp_evac[dc] = cur("V")
        M["P2_V"] = cur("V")

        @block.scalar
        def _(s):
            s.wait_ge(sems["V"], M["P2_V"])
            inc(s.activation(t["r64_P2"][:], t["logdA_P2"][:], AF.Exp, scale=64.0), "S")
        M["R64_S"] = cur("S")

    if stop >= PHASES.index("bloc"):
        ps_pb_t = es.enter_context(nc.psum_tensor("ps_pb", [128, 1024], F32))
        ps_pbe = ps_pb_t[:, 0:512]
        ps_pbo = ps_pb_t[:, 512:1024]
        ps_ys_t = es.enter_context(nc.psum_tensor("ps_ys", [128, 1024], F32))
        ps_ys = [ps_ys_t[:, 0:512], ps_ys_t[:, 512:1024]]

        pb_done = {}
        pb_used = {}
        ctp_used = {}
        ys_half = {}

        for g in range(8):
            for hh in range(2):
                @block.gpsimd
                def _(gp, g=g, hh=hh):
                    gp.wait_ge(sems["LDX"], M["LOGDA_D"])
                    if g == 0 and hh == 0:
                        gp.wait_ge(sems["V"], M["KLOC_V"])
                    elif hh == 0:
                        gp.wait_ge(sems["V"], M[("CTP_V", g - 1)])
                    else:
                        gp.wait_ge(sems["S"], cur("S"))
                    src = t["logdA_dram"].rearrange("d n -> (d n)")[
                        (g * 128 + 64 * hh) * 64:(g * 128 + 64 * hh + 64) * 64]
                    src = src.unsqueeze(0).broadcast_to([128, 4096])
                    inc(gp.dma_start(out=prodbuf, in_=src), "G", 16)

                @block.vector
                def _(v, g=g, hh=hh):
                    v.wait_ge(sems["G"], cur("G"))
                    inc(v.tensor_scalar(prodbuf, prodbuf, JNEG, None, ALU.mult, ALU.bypass), "V")
                M[("PBJ", g, hh)] = cur("V")

                @block.scalar
                def _(s, g=g, hh=hh):
                    s.wait_ge(sems["V"], M[("PBJ", g, hh)])
                    if g >= 1 and hh == 0:
                        s.wait_ge(sems["T"], pb_used[g - 1])
                    inc(s.activation(Pb_g[:, 4096 * hh:4096 * hh + 4096], prodbuf, AF.Exp), "S")
            M[("PB_S", g)] = cur("S")

            @block.vector
            def _(v, g=g):
                v.wait_ge(sems["S"], M[("PB_S", g)])
                v.wait_ge(sems["S"], M["R64_S"])
                if g >= 1:
                    v.wait_ge(sems["T"], ctp_used[g - 1])
                ld = t["logdA_P2"][:, g * 64:(g + 1) * 64]
                in0 = ld.unsqueeze(2).broadcast_to([128, 64, 64])
                in1 = t["iramp1"][:].unsqueeze(1).broadcast_to([128, 64, 64])
                inc(v.tensor_tensor(prodbuf.rearrange("p (pr i) -> p pr i", i=64),
                                    in0, in1, ALU.mult), "V")

            @block.scalar
            def _(s, g=g):
                s.wait_ge(sems["V"], cur("V"))
                inc(s.activation(prodbuf, prodbuf, AF.Exp), "S")

            @block.vector
            def _(v, g=g):
                v.wait_ge(sems["S"], cur("S"))
                ct = t["Ct_P2"][:, g * 64:(g + 1) * 64]
                inc(v.tensor_tensor(CtP_g.rearrange("p (pr i) -> p pr i", i=64),
                                    prodbuf.rearrange("p (pr i) -> p pr i", i=64),
                                    ct.unsqueeze(2).broadcast_to([128, 64, 64]), ALU.mult), "V")
            M[("CTP_V", g)] = cur("V")

            for qq in range(4):
                @block.tensor
                def _(tt, g=g, qq=qq):
                    tt.wait_ge(sems["S"], M[("PB_S", g)])
                    tt.wait_ge(sems["V"], M["XPT_MASKED"])
                    if (g, qq) != (0, 0):
                        prev = (g, qq - 1) if qq > 0 else (g - 1, 3)
                        tt.wait_ge(sems["V"], pb_done[prev])
                    xv = xpT.rearrange("p (c d) -> p c d", d=D)
                    for pr in range(16):
                        for dpar in range(2):
                            din = qq * 32 + pr * 2 + dpar
                            d = g * 128 + din
                            inc(tt.matmul(ps_pbe[64 * dpar:64 * dpar + 64, pr * 32:(pr + 1) * 32],
                                          Pb_g[0:64, din * 64:(din + 1) * 64],
                                          xv[0:64, :, d], start=True, stop=True,
                                          tile_position=(0, 64 * dpar)), "T")
                            inc(tt.matmul(ps_pbo[64 * dpar:64 * dpar + 64, pr * 32:(pr + 1) * 32],
                                          Pb_g[64:128, din * 64:(din + 1) * 64],
                                          xv[64:128, :, d], start=True, stop=True,
                                          tile_position=(64, 64 * dpar)), "T")
                    if qq == 3:
                        pb_used[g] = cur("T")

                @block.vector
                def _(v, g=g, qq=qq):
                    v.wait_ge(sems["T"], cur("T"))
                    b3 = B_sb.rearrange("p (pr c two) -> p pr c two", c=32, two=2)
                    inc(v.tensor_copy(b3[:, :, :, 0],
                                      ps_pbe.rearrange("p (pr c) -> p pr c", c=32)), "V")
                    inc(v.tensor_copy(b3[:, :, :, 1],
                                      ps_pbo.rearrange("p (pr c) -> p pr c", c=32)), "V")
                    pb_done[(g, qq)] = cur("V")
                    v.drain()
                    for pr in range(16):
                        prg = qq * 16 + pr
                        inc(v.tensor_tensor_scan(
                            S_g[:, prg * 64:(prg + 1) * 64],
                            t["r64_P2"][:, g * 64 + prg:g * 64 + prg + 1].broadcast_to([128, 64]),
                            B_sb[:, pr * 64:(pr + 1) * 64],
                            0.0, ALU.mult, ALU.add), "V")
            M[("SCAN_V", g)] = cur("V")

            for half in range(2):
                @block.tensor
                def _(tt, g=g, half=half):
                    tt.wait_ge(sems["V"], M[("SCAN_V", g)])
                    tt.wait_ge(sems["V"], M[("CTP_V", g)])
                    if (g, half) != (0, 0):
                        prev = (g, 0) if half == 1 else (g - 1, 1)
                        tt.wait_ge(sems["V"], ys_half[prev])
                    for prh in range(32):
                        pr = half * 32 + prh
                        for dpar in range(2):
                            lhsT = CtP_g[64 * dpar:64 * dpar + 64, pr * 64:(pr + 1) * 64]
                            Sr = S_g[64 * dpar:64 * dpar + 64, pr * 64:(pr + 1) * 64]
                            mve = Sr[:, 31:63].rearrange("n (c two) -> n c two", two=2)[:, :, 0]
                            mvo = Sr[:, 32:64].rearrange("n (c two) -> n c two", two=2)[:, :, 0]
                            inc(tt.matmul(ps_ys[dpar][0:64, prh * 16:(prh + 1) * 16],
                                          lhsT, mve, start=True, stop=True,
                                          tile_position=(64 * dpar, 0)), "T")
                            inc(tt.matmul(ps_win[dpar][64:128, prh * 16:(prh + 1) * 16],
                                          lhsT, mvo, start=True, stop=True,
                                          tile_position=(64 * dpar, 64)), "T")
                    if half == 1:
                        ctp_used[g] = cur("T")

                @block.vector
                def _(v, g=g, half=half):
                    v.wait_ge(sems["T"], cur("T"))
                    for dpar in range(2):
                        d0 = g * 128 + 64 * half + dpar
                        dhi0 = d0 // 2
                        yte = y_T[0:64, :].rearrange("r (c dhi two) -> r dhi two c",
                                                     two=2, dhi=512)[:, dhi0:dhi0 + 32, dpar, :]
                        srce = ps_ys[dpar][0:64, :].rearrange("r (s c) -> r s c", c=16)
                        inc(v.tensor_copy(yte, srce), "V")
                        yto = y_T[64:128, :].rearrange("r (c dhi two) -> r dhi two c",
                                                       two=2, dhi=512)[:, dhi0:dhi0 + 32, dpar, :]
                        srco = ps_win[dpar][64:128, :].rearrange("r (s c) -> r s c", c=16)
                        inc(v.tensor_copy(yto, srco), "V")
                    ys_half[(g, half)] = cur("V")
        M["YT_DONE"] = cur("V")
        M["BLOC_T"] = cur("T")

    if stop >= PHASES.index("local"):
        xb_evac = {}
        for c in range(16):
            for db in range(NDC):
                k = c * NDC + db
                bank = k % 2

                @block.tensor
                def _(tt, c=c, db=db, k=k, bank=bank):
                    tt.wait_ge(sems["T"], M["BLOC_T"])
                    tt.wait_ge(sems["V"], M["YT_DONE"])
                    if k >= 2:
                        tt.wait_ge(sems["V"], xb_evac[k - 2])
                    inc(tt.transpose(ps_tr[bank][:, 0:128],
                                     xpT[:, (16 + c) * 1024 + db * 128:(16 + c) * 1024 + (db + 1) * 128],
                                     t["ident_bf"][:]), "T")

                @block.vector
                def _(v, c=c, db=db, k=k, bank=bank):
                    v.wait_ge(sems["T"], cur("T"))
                    inc(v.tensor_copy(xpdt[:, db * 2048 + c * 128: db * 2048 + (c + 1) * 128],
                                      ps_tr[bank][:, 0:128]), "V")
                    xb_evac[k] = cur("V")
        M["XPDT_V"] = cur("V")

        ytb_evac = {}
        ycast = {}
        for dc in range(NDC):
            @block.vector
            def _(v, dc=dc):
                v.wait_ge(sems["V"], M["XPDT_V"])
                xsl = xpdt[:, dc * 2048:(dc + 1) * 2048]
                inc(v.tensor_scalar(y_dc, xsl, pv("dpk0", dc), None, ALU.mult, ALU.bypass), "V")
                v.drain()
                y4 = y_dc.rearrange("p (c i) -> p c i", i=64)
                x4 = xsl.rearrange("p (c i) -> p c i", i=64)
                for m in range(1, 64):
                    inc(v.scalar_tensor_tensor(y4[:, :, m:64], x4[:, :, 0:64 - m],
                                               t["k_loc"][:, dc * Q + m:dc * Q + m + 1],
                                               y4[:, :, m:64], ALU.mult, ALU.add), "V")
                    v.drain()
            M[("TAPS", dc)] = cur("V")

            for c in range(16):
                bank = c % 2

                @block.tensor
                def _(tt, dc=dc, c=c, bank=bank):
                    tt.wait_ge(sems["V"], M[("TAPS", dc)])
                    key = (dc, c - 2) if c >= 2 else ((dc - 1, c + 14) if dc > 0 else None)
                    if key is not None:
                        tt.wait_ge(sems["V"], ytb_evac[key])
                    inc(tt.transpose(ps_tr[bank][:, 128:256],
                                     y_T[:, c * 1024 + dc * 128: c * 1024 + (dc + 1) * 128],
                                     t["ident_bf"][:]), "T")

                @block.vector
                def _(v, dc=dc, c=c, bank=bank):
                    v.wait_ge(sems["T"], cur("T"))
                    dst = y_dc[:, c * 128:(c + 1) * 128]
                    inc(v.tensor_tensor(dst, dst, ps_tr[bank][:, 128:256], ALU.add), "V")
                    ytb_evac[(dc, c)] = cur("V")

            @block.vector
            def _(v, dc=dc):
                v.drain()
                xsl = xpdt[:, dc * 2048:(dc + 1) * 2048]
                inc(v.tensor_copy(xsl, y_dc), "V")
                v.drain()
            ycast[dc] = cur("V")
        M["Y_V"] = cur("V")

    if stop >= PHASES.index("router"):
        @block.sync
        def _(sy):
            sy.wait_ge(sems["T"], cur("T"))
            sy.wait_ge(sems["V"], M["Y_V"])
            inc(sy.dma_start(out=acc.rearrange("p (dc k) -> p dc k", k=TOWN),
                             in_=t["xoT"].rearrange("(dc p) k -> p dc k", p=128)), "LDX", 16)
        M["ACC_LD"] = cur("LDX")

        @block.tensor
        def _(tt):
            tt.wait_ge(sems["LDX"], M["ACC_LD"])
            for dc in range(NDC):
                inc(tt.matmul(ps_win[0][0:8, 0:1], t["WrT_sb"][:, dc * E:(dc + 1) * E],
                              ONES, start=(dc == 0), stop=(dc == NDC - 1)), "T")

        @block.vector
        def _(v):
            v.wait_ge(sems["T"], cur("T"))
            inc(v.tensor_copy(t["wsum_sb"][:], ps_win[0][0:8, 0:1]), "V")
            v.drain()
            inc(v.tensor_scalar(t["wsum_diag"][:], t["ident8"][:], t["wsum_sb"][:, 0:1], None,
                                ALU.mult, ALU.bypass), "V")
        M["WSUMD"] = cur("V")

        @block.tensor
        def _(tt):
            tt.wait_ge(sems["V"], M["WSUMD"])
            inc(tt.matmul(ps_win[0][:, 8:16], t["ones8"][:], t["wsum_diag"][:],
                          start=True, stop=True), "T")

        @block.vector
        def _(v):
            v.wait_ge(sems["T"], cur("T"))
            inc(v.tensor_copy(t["wsum_bc"][:], ps_win[0][:, 8:16]), "V")
        M["WSUM_BC"] = cur("V")

        lr_evac = {}
        for tt4 in range(4):
            @block.tensor
            def _(tt, tt4=tt4):
                if tt4 >= 1:
                    tt.wait_ge(sems["V"], lr_evac[tt4 - 1])
                else:
                    tt.wait_ge(sems["V"], M["WSUM_BC"])
                for dc in range(NDC):
                    mv = acc[:, dc * 2048 + tt4 * 512: dc * 2048 + tt4 * 512 + 512]
                    inc(tt.matmul(ps_win[1][0:8, 0:512], t["WrT_sb"][:, dc * E:(dc + 1) * E],
                                  mv, start=(dc == 0), stop=(dc == NDC - 1)), "T")

            @block.vector
            def _(v, tt4=tt4):
                v.wait_ge(sems["T"], cur("T"))
                inc(v.tensor_copy(lraw[:, tt4 * 512:(tt4 + 1) * 512], ps_win[1][0:8, 0:512]), "V")
                lr_evac[tt4] = cur("V")
        M["LRAW"] = cur("V")

        rt_evac = {}
        for tk in range(16):
            bank = tk % 2

            @block.tensor
            def _(tt, tk=tk, bank=bank):
                tt.wait_ge(sems["V"], M["LRAW"])
                if tk >= 2:
                    tt.wait_ge(sems["V"], rt_evac[tk - 2])
                inc(tt.transpose(ps_win[bank][0:128, 0:8],
                                 lraw[:, tk * 128:(tk + 1) * 128], t["ident8"][:]), "T")

            @block.vector
            def _(v, tk=tk, bank=bank):
                v.wait_ge(sems["T"], cur("T"))
                inc(v.tensor_copy(raw_tok[:, tk * 8:(tk + 1) * 8], ps_win[bank][0:128, 0:8]), "V")
                rt_evac[tk] = cur("V")

        r3 = lambda x: x.rearrange("p (k e) -> p k e", e=8)
        rmv = lambda a: rm_v[:, a * 16:(a + 1) * 16]
        bc8 = lambda x: x.unsqueeze(2).broadcast_to([128, 16, 8])

        @block.vector
        def _(v):
            v.wait_ge(sems["V"], cur("V"))
            rstd_own = t["rstd_all"][:, 16:32]
            mean_own = t["mvall"][:].rearrange("p (k two) -> p k two", two=2)[:, 16:32, 0]
            inc(v.tensor_tensor(rmv(0), mean_own, rstd_own, ALU.mult), "V")
            v.drain()
            inc(v.tensor_tensor(r3(nl_v), r3(raw_tok), bc8(rstd_own), ALU.mult), "V")
            v.drain()
            wb = t["wsum_bc"][:].unsqueeze(1).broadcast_to([128, 16, 8])
            inc(v.tensor_tensor(r3(t2_v), wb, bc8(rmv(0)), ALU.mult), "V")
            v.drain()
            inc(v.tensor_tensor(r3(nl_v), r3(t2_v), r3(nl_v), ALU.subtract), "V")
            v.drain()
            inc(v.tensor_reduce(rmv(1), r3(nl_v), AX.X, ALU.min), "V")
            v.drain()
            inc(v.tensor_tensor(r3(ez_v), bc8(rmv(1)), r3(nl_v), ALU.subtract), "V")
        M["RZ"] = cur("V")

        @block.scalar
        def _(s):
            s.wait_ge(sems["V"], M["RZ"])
            inc(s.activation(ez_v, ez_v, AF.Exp), "S")
        M["REZ"] = cur("S")

        @block.vector
        def _(v):
            v.wait_ge(sems["S"], M["REZ"])
            inc(v.tensor_reduce(rmv(4), r3(ez_v), AX.X, ALU.add), "V")
            inc(v.tensor_reduce(rmv(2), r3(ez_v), AX.X, ALU.max), "V")
            v.drain()
            inc(v.tensor_tensor(r3(msk1_v), r3(ez_v), bc8(rmv(2)), ALU.subtract), "V")
            v.drain()
            inc(v.tensor_scalar(msk1_v, msk1_v, 0.0, None, ALU.is_ge, ALU.bypass), "V")
            v.drain()
            inc(v.scalar_tensor_tensor(s2_v, msk1_v, 2.0, ez_v, ALU.mult, ALU.subtract), "V")
            v.drain()
            inc(v.tensor_reduce(rmv(3), r3(s2_v), AX.X, ALU.min), "V")
            v.drain()
            inc(v.tensor_tensor(r3(msk2_v), r3(s2_v), bc8(rmv(3)), ALU.subtract), "V")
            v.drain()
            inc(v.tensor_scalar(msk2_v, msk2_v, 0.0, None, ALU.is_le, ALU.bypass), "V")
            v.drain()
            inc(v.scalar_tensor_tensor(rmv(5), rmv(4), EPS_TOP, rmv(2), ALU.mult, ALU.add), "V")
            v.drain()
            inc(v.tensor_tensor(rmv(5), rmv(5), rmv(3), ALU.subtract), "V")
            v.drain()
            inc(v.reciprocal(rmv(5), rmv(5)), "V")
            v.drain()
            inc(v.tensor_tensor(r3(comb_v), r3(msk1_v), bc8(rmv(2)), ALU.mult), "V")
            inc(v.tensor_tensor(r3(t2_v), r3(msk2_v), bc8(rmv(3)), ALU.mult), "V")
            v.drain()
            inc(v.tensor_tensor(comb_v, comb_v, t2_v, ALU.subtract), "V")
            v.drain()
            inc(v.tensor_tensor(r3(comb_v), r3(comb_v), bc8(rmv(5)), ALU.mult), "V")
        M["COMB"] = cur("V")

        ct_evac = {}
        for tk in range(16):
            bank = tk % 2

            @block.tensor
            def _(tt, tk=tk, bank=bank):
                tt.wait_ge(sems["V"], M["COMB"])
                if tk >= 2:
                    tt.wait_ge(sems["V"], ct_evac[tk - 2])
                inc(tt.transpose(ps_win[bank][0:8, 0:128],
                                 comb_v[:, tk * 8:(tk + 1) * 8], t["ident_f"][:]), "T")

            @block.vector
            def _(v, tk=tk, bank=bank):
                v.wait_ge(sems["T"], cur("T"))
                inc(v.tensor_copy(t["comb_T"][:, tk * 128:(tk + 1) * 128],
                                  ps_win[bank][0:8, 0:128]), "V")
                ct_evac[tk] = cur("V")
        M["COMB_T"] = cur("V")

    if stop >= PHASES.index("wout"):
        @block.gpsimd
        def _(g):
            g.wait_ge(sems["T"], M["BLOC_T"])
            g.wait_ge(sems["V"], M["Y_V"])
            inc(g.dma_start(out=Wout_bf.rearrange("p (dc o) -> p dc o", o=D),
                            in_=t["WoutT"].rearrange("(dc p) o -> p dc o", p=128)), "LDW", 16)
        M["WOUT_LD"] = cur("LDW")

        wo_evac = {}
        for tt4 in range(4):
            for doutc in range(NDC):
                k = tt4 * NDC + doutc
                bank = k % 2

                @block.tensor
                def _(tt, tt4=tt4, doutc=doutc, k=k, bank=bank):
                    tt.wait_ge(sems["LDW"], M["WOUT_LD"])
                    tt.wait_ge(sems["V"], M["Y_V"])
                    tt.wait_ge(sems["LDX"], M["ACC_LD"])
                    if k >= 2:
                        tt.wait_ge(sems["V"], wo_evac[k - 2])
                    for dc in range(NDC):
                        mv = xpdt[:, dc * 2048 + tt4 * 512: dc * 2048 + tt4 * 512 + 512]
                        inc(tt.matmul(ps_win[bank],
                                      Wout_bf[:, dc * D + doutc * 128: dc * D + (doutc + 1) * 128],
                                      mv, start=(dc == 0), stop=(dc == NDC - 1)), "T")

                @block.vector
                def _(v, tt4=tt4, doutc=doutc, k=k, bank=bank):
                    v.wait_ge(sems["T"], cur("T"))
                    a = acc[:, doutc * 2048 + tt4 * 512: doutc * 2048 + tt4 * 512 + 512]
                    inc(v.scalar_tensor_tensor(a, ps_win[bank], pv("bout", doutc),
                                               a, ALU.add, ALU.add), "V")
                    wo_evac[k] = cur("V")
        M["WOUT_V"] = cur("V")
        M["WOUT_T"] = cur("T")

    if stop >= PHASES.index("moe"):
        @block.gpsimd
        def _(gp):
            gp.wait_ge(sems["V"], M["COMB_T"])
            gp.wait_ge(sems["T"], M["WOUT_T"])
            inc(gp.dma_start(out=sel_bf, in_=t["sel_c"]), "LDW", 16)
        M["SEL_LD"] = cur("LDW")

        ps_trf = ps_tr_t.bitcast(F32)
        BSET = [[ps_win[0], ps_win[1], ps_trf[:, 0:512], ps_trf[:, 512:1024]],
                [ps_ys[0], ps_ys[1], ps_pbe, ps_pbo]]

        wg_used = {}
        wd_used = {}
        hwaitv = {}
        h_done = {}
        gr_done = {}
        cb_evac = {}
        mo_evac = {}

        for e in range(E):
            for tt4 in range(4):
                @block.tensor
                def _(tt, e=e, tt4=tt4):
                    tt.wait_ge(sems["LDW"], M["SEL_LD"])
                    if (e, tt4) != (0, 0):
                        prev = (e, tt4 - 1) if tt4 > 0 else (e - 1, 3)
                        tt.wait_ge(sems["V"], cb_evac[prev])
                    if e > 0:
                        tt.wait_ge(sems["V"], M[("A_DONE", e - 1, 3)])
                        tt.wait_ge(sems["V"], mo_evac[(e - 1, 3, 15)])
                    inc(tt.matmul(ps_pbo[:, 0:512], sel_bf[:, e * 128:(e + 1) * 128],
                                  t["comb_T"][:, tt4 * 512:(tt4 + 1) * 512],
                                  start=True, stop=True), "T")

                @block.vector
                def _(v, e=e, tt4=tt4):
                    v.wait_ge(sems["T"], cur("T"))
                    inc(v.tensor_copy(comb_bc[:, tt4 * 512:(tt4 + 1) * 512], ps_pbo[:, 0:512]), "V")
                    cb_evac[(e, tt4)] = cur("V")
            M[("CB", e)] = cur("V")

            for qtr in range(4):
                for sub in range(2):
                    ei = qtr * 2 + sub
                    slot = ei % 2

                    @block.gpsimd
                    def _(gp, e=e, ei=ei, slot=slot):
                        if e == 0 and ei < 2:
                            gp.wait_ge(sems["T"], M["WOUT_T"])
                            gp.wait_ge(sems["V"], M["WOUT_V"])
                        key = (e, ei - 2) if ei >= 2 else ((e - 1, 6 + ei) if e > 0 else None)
                        if key is not None:
                            gp.wait_ge(sems["T"], wg_used[key])
                        src = t["WgT"][e, :, ei * 512:(ei + 1) * 512]
                        inc(gp.dma_start(out=Wg_bf(slot).rearrange("p (dc f) -> p dc f", f=512),
                                         in_=src.rearrange("(dc p) f -> p dc f", p=128)), "LDW", 16)
                        src2 = t["WuT"][e, :, ei * 512:(ei + 1) * 512]
                        inc(gp.dma_start(out=Wu_bf(slot).rearrange("p (dc f) -> p dc f", f=512),
                                         in_=src2.rearrange("(dc p) f -> p dc f", p=128)), "LDW", 16)
                    M[("WGDMA", e, ei)] = cur("LDW")

                @block.gpsimd
                def _(gp, e=e, qtr=qtr):
                    if e == 0 and qtr == 0:
                        gp.wait_ge(sems["T"], M["WOUT_T"])
                        gp.wait_ge(sems["V"], M["WOUT_V"])
                    key = (e, qtr - 1) if qtr >= 1 else ((e - 1, 3) if e > 0 else None)
                    if key is not None:
                        gp.wait_ge(sems["T"], wd_used[key])
                    src = t["WdT"][e, qtr * 1024:(qtr + 1) * 1024, :]
                    inc(gp.dma_start(out=Wd_bf.rearrange("p (q o) -> p q o", o=D),
                                     in_=src.rearrange("(q p) o -> p q o", p=128)), "LDW", 16)
                M[("WDDMA", e, qtr)] = cur("LDW")

                for sub in range(2):
                    ei = qtr * 2 + sub
                    slot = ei % 2
                    for kk2 in range(8):
                        tt2 = kk2 // 4
                        dffc = kk2 % 4
                        par = kk2 % 2
                        dffq = sub * 4 + dffc
                        SB_ = BSET[par]

                        @block.tensor
                        def _(tt, e=e, qtr=qtr, ei=ei, slot=slot, kk2=kk2, tt2=tt2, dffc=dffc, par=par, SB_=SB_):
                            tt.wait_ge(sems["LDW"], M[("WGDMA", e, ei)])
                            if ei == 0 and kk2 <= 1:
                                tt.wait_ge(sems["V"], M[("CB", e)])
                            if kk2 >= 2:
                                tt.wait_ge(sems["V"], h_done[(e, ei, kk2 - 2)])
                            elif ei % 2 == 1:
                                tt.wait_ge(sems["V"], h_done[(e, ei - 1, 6 + kk2)])
                            else:
                                pq = (e, qtr - 1) if qtr > 0 else ((e - 1, 3) if e > 0 else None)
                                if pq is not None:
                                    tt.wait_ge(sems["V"], mo_evac[(pq[0], pq[1], 14 + kk2)])
                            for tb in range(2):
                                W = Wg_bf(slot) if tb == 0 else Wu_bf(slot)
                                for dc in range(NDC):
                                    st0 = W[:, dc * 512 + dffc * 128: dc * 512 + (dffc + 1) * 128]
                                    for j in range(2):
                                        tt4 = 2 * tt2 + j
                                        mv = t["xn_T"][:, dc * 2048 + tt4 * 512: dc * 2048 + tt4 * 512 + 512]
                                        inc(tt.matmul(SB_[2 * tb + j], st0, mv,
                                                      start=(dc == 0), stop=(dc == NDC - 1)), "T")
                            if kk2 == 7:
                                wg_used[(e, ei)] = cur("T")
                        hwaitv[(e, ei, kk2)] = None

                        @block.scalar
                        def _(s, e=e, ei=ei, kk2=kk2, par=par, SB_=SB_):
                            s.wait_ge(sems["T"], cur("T") - 16)
                            if kk2 >= 2:
                                s.wait_ge(sems["V"], h_done[(e, ei, kk2 - 2)])
                            inc(s.activation(gr_sb(2 * par), SB_[0], AF.Relu), "S")
                            inc(s.activation(gr_sb(2 * par + 1), SB_[1], AF.Relu), "S")
                            gr_done[(e, ei, kk2)] = cur("S")

                        @block.vector
                        def _(v, e=e, ei=ei, kk2=kk2, tt2=tt2, par=par, SB_=SB_, dffq=dffq):
                            v.wait_ge(sems["T"], cur("T"))
                            v.wait_ge(sems["S"], gr_done[(e, ei, kk2)])
                            h0 = h_all[:, (dffq * 4 + 2 * tt2) * 512:(dffq * 4 + 2 * tt2) * 512 + 512]
                            h1 = h_all[:, (dffq * 4 + 2 * tt2 + 1) * 512:(dffq * 4 + 2 * tt2 + 1) * 512 + 512]
                            inc(v.tensor_tensor(h0, gr_sb(2 * par), SB_[2], ALU.mult), "V")
                            inc(v.tensor_tensor(h1, gr_sb(2 * par + 1), SB_[3], ALU.mult), "V")
                            v.drain()
                            inc(v.tensor_tensor(h0, h0, comb_bc[:, 2 * tt2 * 512:(2 * tt2 + 1) * 512], ALU.mult), "V")
                            inc(v.tensor_tensor(h1, h1, comb_bc[:, (2 * tt2 + 1) * 512:(2 * tt2 + 2) * 512], ALU.mult), "V")
                            h_done[(e, ei, kk2)] = cur("V")
                M[("A_DONE", e, qtr)] = cur("V")

                for pk in range(16):
                    tt2 = pk // 8
                    doutc = pk % 8
                    Bset = BSET[pk % 2]

                    @block.tensor
                    def _(tt, e=e, qtr=qtr, pk=pk, tt2=tt2, doutc=doutc, Bset=Bset):
                        tt.wait_ge(sems["LDW"], M[("WDDMA", e, qtr)])
                        tt.wait_ge(sems["V"], M[("A_DONE", e, qtr)])
                        key = (e, qtr, pk - 2)
                        if key in mo_evac:
                            tt.wait_ge(sems["V"], mo_evac[key])
                        else:
                            pko = (e, qtr - 1, 14 + pk) if qtr > 0 else \
                                  ((e - 1, 3, 14 + pk) if e > 0 else None)
                            if pko is not None:
                                tt.wait_ge(sems["V"], mo_evac[pko])
                        for dffq in range(8):
                            st0 = Wd_bf[:, dffq * D + doutc * 128: dffq * D + (doutc + 1) * 128]
                            for j in range(2):
                                tt4 = 2 * tt2 + j
                                mv = h_all[:, (dffq * 4 + tt4) * 512:(dffq * 4 + tt4) * 512 + 512]
                                inc(tt.matmul(Bset[2 * j], st0, mv,
                                              start=(dffq == 0), stop=(dffq == 7)), "T")
                        if pk == 15:
                            wd_used[(e, qtr)] = cur("T")

                    @block.vector
                    def _(v, e=e, qtr=qtr, pk=pk, tt2=tt2, doutc=doutc, Bset=Bset):
                        v.wait_ge(sems["T"], cur("T"))
                        for j in range(2):
                            tt4 = 2 * tt2 + j
                            a = acc[:, doutc * 2048 + tt4 * 512: doutc * 2048 + tt4 * 512 + 512]
                            inc(v.tensor_tensor(a, a, Bset[2 * j], ALU.add), "V")
                        mo_evac[(e, qtr, pk)] = cur("V")
        M["MOE_V"] = cur("V")

    if stop >= PHASES.index("store"):
        st_evac = {}
        for tk in range(16):
            for dc in range(NDC):
                k = tk * NDC + dc
                bank = k % 2

                @block.tensor
                def _(tt, tk=tk, dc=dc, k=k, bank=bank):
                    tt.wait_ge(sems["V"], M.get("MOE_V", M.get("WOUT_V", cur("V"))))
                    if k >= 2:
                        tt.wait_ge(sems["V"], st_evac[k - 2])
                    inc(tt.transpose(ps_win[bank][:, 0:128],
                                     acc[:, dc * 2048 + tk * 128: dc * 2048 + (tk + 1) * 128],
                                     t["ident_f"][:]), "T")

                @block.vector
                def _(v, tk=tk, dc=dc, k=k, bank=bank):
                    v.wait_ge(sems["T"], cur("T"))
                    if dc == 0 and tk >= 1:
                        v.wait_ge(sems["ST"], 16 * tk)
                    inc(v.tensor_copy(out_stg[:, dc * 128:(dc + 1) * 128],
                                      ps_win[bank][:, 0:128]), "V")
                    st_evac[k] = cur("V")

            @block.sync
            def _(sy, tk=tk):
                sy.wait_ge(sems["V"], st_evac[tk * NDC + NDC - 1])
                inc(sy.dma_start(out=t["out_tok"][tk * 128:(tk + 1) * 128, :], in_=out_stg), "ST", 16)

        @block.sync
        def _(sy):
            sy.wait_ge(sems["ST"], cur("ST"))

    dbg_views = {
        "xn_T": t["xn_T"][:], "xpT": xpT, "xpdt": xpdt, "mvall": t["mvall"][:],
        "rstd_all": t["rstd_all"][:], "k_loc": t["k_loc"][:], "logdA": logdA_v,
        "Ct": Ct_v, "logdA_P2": t["logdA_P2"][:], "Ct_P2": t["Ct_P2"][:],
        "r64_P2": t["r64_P2"][:], "Pb_g": Pb_g, "CtP_g": CtP_g, "S_g": S_g,
        "y_T": y_T, "B_sb": B_sb, "acc": acc, "comb_T": t["comb_T"][:],
        "comb": comb_v, "ez": ez_v, "raw_tok": raw_tok, "lraw": lraw,
        "wsum_bc": t["wsum_bc"][:], "h_all": h_all, "comb_bc": comb_bc,
        "y_dc": y_dc, "dpk0": t["pvec"][:, 32:40], "xn_tok": xn_tok,
    }
    if debug_keys:
        dbg_out = {}
        for name in debug_keys:
            ap = dbg_views[name]
            dbg_out[name] = nc.dram_tensor("dbg_" + name, list(ap.shape), ap.dtype,
                                           kind="ExternalOutput").ap()

        @block.sync
        def _(sy):
            for s in ["V", "S", "T", "G"]:
                sy.wait_ge(sems[s], cur(s))
            for name in debug_keys:
                inc(sy.dma_start(out=dbg_out[name], in_=dbg_views[name]), "ST", 16)
            sy.wait_ge(sems["ST"], cur("ST"))

    es.close()
    return nc



import ml_dtypes
BF = ml_dtypes.bfloat16



def reshape_dvec(v):
    return np.ascontiguousarray(v.reshape(NDC, 128).T)


def reshape_dmat(m):
    K = m.shape[1]
    return np.ascontiguousarray(m.reshape(NDC, 128, K).transpose(1, 0, 2).reshape(128, NDC * K))


def consts():
    c = {}
    c["ident_c"] = np.eye(128, dtype=np.float32)
    sel = np.zeros((8, E * 128), dtype=np.float32)
    for e in range(E):
        sel[e, e * 128:(e + 1) * 128] = 1.0
    c["sel_c"] = sel
    c["ones8_c"] = np.ones((8, 128), dtype=np.float32)
    c["ident8_c"] = np.eye(8, dtype=np.float32)
    smalls = np.zeros((128, 16), dtype=np.float32)
    smalls[:, 0] = 1.0
    smalls[:, 1] = 63.0 - (np.arange(128) % 64)
    c["smalls_c"] = smalls
    c["iramp1_c"] = np.tile(np.arange(1, 65, dtype=np.float32), (128, 1))
    c["mramp_c"] = np.tile(np.arange(64, dtype=np.float32), (128, 1))
    return c


def core_inputs(inputs, core):
    b, h = core // 2, core % 2
    x = np.asarray(inputs["L0_attn_x"])
    own = x[b, h * TOWN:(h + 1) * TOWN, :]
    prev = x[b, 0:TOWN, :] if h == 1 else np.zeros((TOWN, D), np.float32)
    m = {}
    m["xf"] = np.ascontiguousarray(np.concatenate([prev, own], axis=0))
    m["xoT"] = np.ascontiguousarray(own.T)
    m["cmask"] = np.full((128, 1), float(h), np.float32)
    m["WinT"] = np.ascontiguousarray(np.asarray(inputs["L0_attn_Win"]).T)
    m["binb"] = reshape_dvec(np.asarray(inputs["L0_attn_bin_"]))
    m["WoutT"] = np.ascontiguousarray(np.asarray(inputs["L0_attn_Wout"]).T)
    m["boutb"] = reshape_dvec(np.asarray(inputs["L0_attn_bout"]))
    m["A_log_r"] = reshape_dmat(np.asarray(inputs["L0_attn_A_log"]))
    m["B_r"] = reshape_dmat(np.asarray(inputs["L0_attn_B"]))
    m["C_r"] = reshape_dmat(np.asarray(inputs["L0_attn_C"]))
    m["dt_r"] = reshape_dvec(np.asarray(inputs["L0_attn_dt"]))
    m["Dp_r"] = reshape_dvec(np.asarray(inputs["L0_attn_Dp"]))
    m["WrT_r"] = reshape_dmat(np.ascontiguousarray(np.asarray(inputs["L0_ffn_Wr"]).T))
    m["WgT"] = np.ascontiguousarray(np.asarray(inputs["L0_ffn_Wg"]).transpose(0, 2, 1))
    m["WuT"] = np.ascontiguousarray(np.asarray(inputs["L0_ffn_Wu"]).transpose(0, 2, 1))
    m["WdT"] = np.ascontiguousarray(np.asarray(inputs["L0_ffn_Wd"]).transpose(0, 2, 1))
    m.update(consts())
    return m




_CACHE = {}


def kernel(**inputs):
    import numpy as np
    from concourse.bass_utils import run_bass_kernel_spmd
    if "nc" not in _CACHE:
        _CACHE["nc"] = build()
    nc = _CACHE["nc"]
    cores = list(range(8))
    maps = [core_inputs(inputs, c) for c in cores]
    res = run_bass_kernel_spmd(nc, maps, core_ids=cores)
    out = np.zeros((4, TFULL, D), np.float32)
    for ci, c in enumerate(cores):
        b, h = c // 2, c % 2
        out[b, h * TOWN:(h + 1) * TOWN, :] = np.asarray(res.results[ci]["out_tok"])
    return out


# version 04 (final)
# speedup vs baseline: 1.3149x; 1.0574x over previous
import contextlib
from collections import defaultdict

import numpy as np
import concourse.bass as bass
import concourse.mybir as mybir

F32 = mybir.dt.float32
BF16 = mybir.dt.bfloat16
AF = mybir.ActivationFunctionType
ALU = mybir.AluOpType
AX = mybir.AxisListType

D = 1024
NST = 64
TOWN = 2048
TFULL = 4096
Q = 64
NDC = 8
E = 8
DFF = 4096
EPS_LN = 1e-5
EPS_TOP = float(np.float32(1.1920929e-07))

PHASES = ["ln_win", "conv_params", "bloc", "local", "router", "wout", "moe", "store"]


def build(debug_keys=(), upto="store", ln_sub="full"):
    stop = PHASES.index(upto)
    nc = bass.Bass("TRN2", target_bir_lowering=False)
    es = contextlib.ExitStack()
    t = {}
    cnt = defaultdict(int)
    debug_keys = list(debug_keys)

    def dri(name, shape, dtype=F32):
        t[name] = nc.dram_tensor(name, list(shape), dtype, kind="ExternalInput").ap()

    def sbt(name, shape, dtype):
        t[name] = es.enter_context(nc.sbuf_tensor(name, list(shape), dtype))

    dri("xf", (TFULL, D))
    dri("xoT", (D, TOWN))
    dri("cmask", (128, 1))
    dri("WinT", (D, D))
    dri("binb", (128, NDC))
    dri("WoutT", (D, D))
    dri("boutb", (128, NDC))
    dri("A_log_r", (128, NDC * NST))
    dri("B_r", (128, NDC * NST))
    dri("C_r", (128, NDC * NST))
    dri("dt_r", (128, NDC))
    dri("Dp_r", (128, NDC))
    dri("WrT_r", (128, NDC * E))
    dri("WgT", (E, D, DFF))
    dri("WuT", (E, D, DFF))
    dri("WdT", (E, DFF, D))
    dri("ident_c", (128, 128))
    dri("sel_c", (8, E * 128))
    dri("ones8_c", (8, 128))
    dri("ident8_c", (8, 8))
    dri("smalls_c", (128, 16))
    dri("iramp1_c", (128, 64))
    dri("mramp_c", (128, 64))
    dri("zeros_c", (128, 4096))
    t["out_tok"] = nc.dram_tensor("out_tok", [TOWN, D], F32, kind="ExternalOutput").ap()
    t["logdA_dram"] = nc.dram_tensor("logdA_dram", [D, NST], F32, kind="Internal").ap()
    t["kshift_dram"] = nc.dram_tensor("kshift_dram", [Q, D, Q], F32, kind="Internal").ap()

    sbt("xn_T", (128, NDC * TOWN), BF16)
    sbt("ident_f", (128, 128), F32)
    sbt("ident_bf", (128, 128), BF16)
    sbt("ones8", (8, 128), F32)
    sbt("ident8", (8, 8), F32)
    sbt("smalls", (128, 16), F32)
    sbt("iramp1", (128, 64), F32)
    sbt("mramp", (128, 64), F32)
    sbt("pvec", (128, 64), F32)
    sbt("WrT_sb", (128, NDC * E), F32)
    sbt("mvall", (128, 64), F32)
    sbt("rstd_all", (128, 32), F32)
    sbt("stats6", (128, 12), F32)
    sbt("k_loc", (128, NDC * Q), F32)
    sbt("logdA_P2", (128, 512), F32)
    sbt("Ct_P2", (128, 512), F32)
    sbt("r64_P2", (128, 512), F32)
    sbt("comb_T", (8, TOWN), BF16)
    sbt("wsum_sb", (8, 1), F32)
    sbt("wsum_diag", (8, 8), F32)
    sbt("wsum_bc", (128, 8), F32)

    sbt("AA", (128, 32768), BF16)
    sbt("AB", (128, 32768), BF16)
    sbt("AC", (128, 16384), BF16)

    def av(arena, byte_off, ncols, dtype):
        assert byte_off % 4 == 0
        o = byte_off // 2
        if dtype == BF16:
            return t[arena][:, o:o + ncols]
        return t[arena][:, o:o + 2 * ncols].bitcast(F32)

    xf_roll = lambda slot: av("AA", 0, 4096, F32)
    xn_tok = av("AA", 16384, 4096, BF16)
    xn_roll = lambda slot: av("AA", 24576 + slot * 8192, 4096, BF16)
    xps = lambda slot: av("AA", 40960 + slot * 1024, 512, BF16)
    Win_bf = av("AA", 43008, 8192, BF16)
    Alog_v = av("AA", 0, 512, F32)
    Bm_v = av("AA", 2048, 512, F32)
    Cm_v = av("AA", 4096, 512, F32)
    logdA_v = av("AA", 6144, 512, F32)
    Ct_v = av("AA", 8192, 512, F32)
    prodbuf = av("AA", 16384, 2048, F32)
    Pb_g = av("AA", 24576, 8192, BF16)
    CtP_g = av("AA", 40960, 4096, BF16)
    B_sb = av("AA", 49152, 1024, F32)
    S_g = av("AA", 53248, 4096, BF16)
    xpdt = av("AA", 0, 16384, BF16)
    y_dc = av("AA", 32768, 2048, F32)
    Wout_bf = av("AA", 40960, 8192, BF16)
    h_all = av("AA", 0, 16384, BF16)
    Wg_bf = lambda slot: av("AA", 32768 + slot * 8192, 4096, BF16)
    Wu_bf = lambda slot: av("AA", 49152 + slot * 8192, 4096, BF16)

    T_bf = lambda slot: av("AA", slot * 8192, 4096, BF16)

    xpT = av("AB", 0, 32768, BF16)
    acc = av("AB", 0, 16384, F32)

    y_T = av("AC", 0, 16384, BF16)
    lraw = av("AC", 0, 2048, F32)[0:8, :]
    raw_tok = av("AC", 8192, 128, F32)
    nl_v = av("AC", 8704, 128, F32)
    ez_v = av("AC", 9216, 128, F32)
    s2_v = av("AC", 9728, 128, F32)
    msk1_v = av("AC", 10240, 128, F32)
    msk2_v = av("AC", 10752, 128, F32)
    t2_v = av("AC", 11264, 128, F32)
    comb_v = av("AC", 11776, 128, F32)
    rm_v = av("AC", 12288, 96, F32)
    Wd_bf = av("AC", 0, 8192, BF16)
    comb_bc = av("AC", 16384, 2048, BF16)
    gr_sb = lambda slot: av("AC", 20480 + slot * 1024, 512, BF16)
    out_stg = av("AC", 22528, 1024, F32)
    sel_bf = av("AC", 26624, 1024, BF16)[0:8, :]

    sems = {}
    for s in ["LDX", "LDW", "T", "V", "S", "G", "ST"]:
        sems[s] = es.enter_context(nc.semaphore(s))
    block = es.enter_context(nc.Block())

    def inc(instr, s, n=1):
        cnt[s] += n
        return instr.then_inc(sems[s], n)

    def cur(s):
        return cnt[s]

    M = {}
    PV = {"bin": 0, "bout": 8, "dt": 16, "Dp": 24, "dpk0": 32, "cm": 40}

    def pv(name, j=0):
        o = PV[name] + j
        return t["pvec"][:, o:o + 1]

    ONES = t["smalls"][:, 0:1]
    JNEG = t["smalls"][:, 1:2]

    @block.sync
    def _(sy):
        for dst, src in [("ident_f", "ident_c"), ("ones8", "ones8_c"), ("ident8", "ident8_c"),
                         ("smalls", "smalls_c"), ("iramp1", "iramp1_c"), ("mramp", "mramp_c"),
                         ("WrT_sb", "WrT_r")]:
            inc(sy.dma_start(out=t[dst][:], in_=t[src]), "LDX", 16)
        inc(sy.dma_start(out=t["pvec"][:, 0:8], in_=t["binb"]), "LDX", 16)
        inc(sy.dma_start(out=t["pvec"][:, 8:16], in_=t["boutb"]), "LDX", 16)
        inc(sy.dma_start(out=t["pvec"][:, 16:24], in_=t["dt_r"]), "LDX", 16)
        inc(sy.dma_start(out=t["pvec"][:, 24:32], in_=t["Dp_r"]), "LDX", 16)
        inc(sy.dma_start(out=t["pvec"][:, 40:41], in_=t["cmask"]), "LDX", 16)
    M["LDX_SMALL"] = cur("LDX")

    @block.gpsimd
    def _(g):
        inc(g.dma_start(out=t["ident_bf"][:], in_=t["ident_c"]), "LDW", 16)
        inc(g.dma_start(out=Win_bf.rearrange("p (dc o) -> p dc o", o=D),
                        in_=t["WinT"].rearrange("(dc p) o -> p dc o", p=128)), "LDW", 16)
    M["LDW_INIT"] = cur("LDW")

    ps_win_t = es.enter_context(nc.psum_tensor("ps_win", [128, 1024], F32))
    ps_win = [ps_win_t[:, 0:512], ps_win_t[:, 512:1024]]
    ps_tr_t = es.enter_context(nc.psum_tensor("ps_tr", [128, 2048], BF16))
    ps_tr = [ps_tr_t[:, 0:1024], ps_tr_t[:, 1024:2048]]

    vslot_free = {}
    xn_evac = {}
    xps_evac = {}
    xpt_evac = {}

    for grp in range(8):
        own = grp >= 4
        slot = 0

        @block.scalar
        def _(s, grp=grp):
            if grp >= 1:
                s.wait_ge(sems["V"], vslot_free[grp - 1])
            src = t["xf"].rearrange("(g q p) d -> g p q d", g=8, p=128)[grp]
            inc(s.dma_start(out=xf_roll(0).rearrange("p (q d) -> p q d", d=D), in_=src),
                "LDX", 16)

        @block.vector
        def _(v, grp=grp):
            v.wait_ge(sems["LDX"], cur("LDX"))
            xr = xf_roll(0)
            for q in range(4):
                tile = 4 * grp + q
                inc(v.bn_stats(t["stats6"][:, 0:6], xr[:, q * 1024:q * 1024 + 512]), "V")
                inc(v.bn_stats(t["stats6"][:, 6:12], xr[:, q * 1024 + 512:(q + 1) * 1024]), "V")
                v.drain()
                inc(v.bn_aggr(t["mvall"][:, 2 * tile:2 * tile + 2], t["stats6"][:]), "V")
                v.drain()
            vs = t["mvall"][:].rearrange("p (k two) -> p k two", two=2)[:, 4 * grp:4 * grp + 4, 1]
            inc(v.tensor_scalar(t["rstd_all"][:, 4 * grp:4 * grp + 4], vs, EPS_LN, None,
                                ALU.add, ALU.bypass), "V")
            v.drain()
            inc(v.reciprocal(t["rstd_all"][:, 4 * grp:4 * grp + 4],
                             t["rstd_all"][:, 4 * grp:4 * grp + 4]), "V")

        @block.scalar
        def _(s, grp=grp):
            s.wait_ge(sems["V"], cur("V"))
            inc(s.sqrt(t["rstd_all"][:, 4 * grp:4 * grp + 4],
                       t["rstd_all"][:, 4 * grp:4 * grp + 4]), "S")

        @block.vector
        def _(v, grp=grp):
            v.wait_ge(sems["S"], cur("S"))
            if grp >= 1 and ("XNTR_T", grp - 1) in M:
                v.wait_ge(sems["T"], M[("XNTR_T", grp - 1)])
            xr = xf_roll(0)
            for q in range(4):
                tile = 4 * grp + q
                inc(v.tensor_scalar(xn_tok[:, q * 1024:(q + 1) * 1024],
                                    xr[:, q * 1024:(q + 1) * 1024],
                                    t["mvall"][:, 2 * tile:2 * tile + 1],
                                    t["rstd_all"][:, tile:tile + 1],
                                    ALU.subtract, ALU.mult), "V")
        vslot_free[grp] = cur("V")
        M[("XN_TOK", grp)] = cur("V")

        for dc in range(NDC if ln_sub != "stats" else 0):
            bank = dc % 2

            @block.tensor
            def _(tt, grp=grp, dc=dc, bank=bank):
                tt.wait_ge(sems["V"], M[("XN_TOK", grp)])
                tt.wait_ge(sems["LDW"], M["LDW_INIT"])
                key = (grp, dc - 2) if dc >= 2 else ((grp - 1, dc + 6) if grp > 0 else None)
                if key is not None:
                    tt.wait_ge(sems["V"], xn_evac[key])
                for q in range(4):
                    inc(tt.transpose(ps_tr[bank][:, q * 128:(q + 1) * 128],
                                     xn_tok[:, q * 1024 + dc * 128:q * 1024 + (dc + 1) * 128],
                                     t["ident_bf"][:]), "T")

            @block.vector
            def _(v, grp=grp, dc=dc, bank=bank, own=own):
                v.wait_ge(sems["T"], cur("T"))
                if own:
                    o0 = dc * 2048 + (grp - 4) * 512
                    dst = t["xn_T"][:, o0:o0 + 512]
                else:
                    if grp >= 2:
                        v.wait_ge(sems["T"], M[("WIN_T", grp - 2)])
                    dst = xn_roll(grp % 2)[:, dc * 512:(dc + 1) * 512]
                inc(v.tensor_copy(dst, ps_tr[bank][:, 0:512]), "V")
                xn_evac[(grp, dc)] = cur("V")
        M[("XN_RDY", grp)] = cur("V")
        M[("XNTR_T", grp)] = cur("T")

        for doutc in range(NDC if ln_sub == "full" else 0):
            bank = doutc % 2

            @block.tensor
            def _(tt, grp=grp, doutc=doutc, bank=bank, own=own):
                tt.wait_ge(sems["V"], M[("XN_RDY", grp)])
                key = (grp, doutc - 2) if doutc >= 2 else ((grp - 1, doutc + 6) if grp > 0 else None)
                if key is not None:
                    tt.wait_ge(sems["S"], xps_evac[key])
                for dc in range(NDC):
                    if own:
                        mv = t["xn_T"][:, dc * 2048 + (grp - 4) * 512: dc * 2048 + (grp - 4) * 512 + 512]
                    else:
                        mv = xn_roll(grp % 2)[:, dc * 512:(dc + 1) * 512]
                    inc(tt.matmul(ps_win[bank],
                                  Win_bf[:, dc * D + doutc * 128: dc * D + (doutc + 1) * 128],
                                  mv, start=(dc == 0), stop=(dc == NDC - 1)), "T")

            @block.scalar
            def _(s, grp=grp, doutc=doutc, bank=bank):
                s.wait_ge(sems["T"], cur("T"))
                key = (grp, doutc - 2) if doutc >= 2 else ((grp - 1, doutc + 6) if grp > 0 else None)
                if key is not None:
                    s.wait_ge(sems["V"], xpt_evac[key])
                inc(s.activation(xps(bank), ps_win[bank], AF.Identity,
                                 bias=pv("bin", doutc)), "S")
                xps_evac[(grp, doutc)] = cur("S")

            @block.tensor
            def _(tt, grp=grp, doutc=doutc, bank=bank):
                tt.wait_ge(sems["S"], cur("S"))
                for q in range(4):
                    inc(tt.transpose(ps_tr[bank][:, 512 + q * 128:512 + (q + 1) * 128],
                                     xps(bank)[:, q * 128:(q + 1) * 128],
                                     t["ident_bf"][:]), "T")

            @block.vector
            def _(v, grp=grp, doutc=doutc, bank=bank):
                v.wait_ge(sems["T"], cur("T"))
                dst = xpT.rearrange("p (c d) -> p c d", d=D)[
                    :, grp * 4:grp * 4 + 4, doutc * 128:(doutc + 1) * 128]
                src = ps_tr[bank][:, 512:1024].rearrange("p (q j) -> p q j", q=4)
                inc(v.tensor_copy(dst, src), "V")
                xpt_evac[(grp, doutc)] = cur("V")
        M[("WIN_T", grp)] = cur("T")

    @block.vector
    def _(v):
        v.wait_ge(sems["LDX"], M["LDX_SMALL"])
        ap = xpT[:, 0:16 * 1024]
        inc(v.tensor_scalar(ap, ap, pv("cm"), None, ALU.mult, ALU.bypass), "V")
    M["XPT_MASKED"] = cur("V")
    M["LN_T"] = cur("T")
    M["LN_V"] = cur("V")

    if stop >= PHASES.index("conv_params"):
        @block.sync
        def _(sy):
            sy.wait_ge(sems["T"], M["LN_T"])
            sy.wait_ge(sems["V"], M["LN_V"])
            inc(sy.dma_start(out=Alog_v, in_=t["A_log_r"]), "LDX", 16)
            inc(sy.dma_start(out=Bm_v, in_=t["B_r"]), "LDX", 16)
            inc(sy.dma_start(out=Cm_v, in_=t["C_r"]), "LDX", 16)
        M["CONVLD"] = cur("LDX")

        @block.scalar
        def _(s):
            s.wait_ge(sems["LDX"], M["CONVLD"])
            inc(s.activation(logdA_v, Alog_v, AF.Exp, scale=-1.0), "S")
        M["EXPA"] = cur("S")

        @block.vector
        def _(v):
            v.wait_ge(sems["S"], M["EXPA"])
            for dc in range(NDC):
                sl = slice(dc * NST, (dc + 1) * NST)
                inc(v.tensor_scalar(logdA_v[:, sl], logdA_v[:, sl],
                                    pv("dt", dc), -1.0, ALU.mult, ALU.mult), "V")
            v.drain()
            inc(v.tensor_tensor(Ct_v, Cm_v, Bm_v, ALU.mult), "V")
            v.drain()
            for dc in range(NDC):
                sl = slice(dc * NST, (dc + 1) * NST)
                inc(v.tensor_scalar(Ct_v[:, sl], Ct_v[:, sl],
                                    pv("dt", dc), None, ALU.mult, ALU.bypass), "V")
            v.drain()
        M["LOGDA_V"] = cur("V")

        @block.sync
        def _(sy):
            sy.wait_ge(sems["V"], M["LOGDA_V"])
            inc(sy.dma_start(out=t["logdA_dram"].rearrange("(dc p) n -> p dc n", p=128),
                             in_=logdA_v.rearrange("p (dc n) -> p dc n", n=NST)), "LDX", 16)
        M["LOGDA_D"] = cur("LDX")

        for dc in range(NDC):
          for mh in range(2):
            @block.vector
            def _(v, dc=dc, mh=mh):
                v.wait_ge(sems["V"], M["LOGDA_V"])
                if (dc, mh) != (0, 0):
                    v.wait_ge(sems["S"], cur("S"))
                ld = logdA_v[:, dc * NST:(dc + 1) * NST]
                in0 = ld.unsqueeze(1).broadcast_to([128, 32, NST])
                in1 = t["mramp"][:, mh * 32:(mh + 1) * 32].unsqueeze(2).broadcast_to([128, 32, NST])
                inc(v.tensor_tensor(prodbuf.rearrange("p (m n) -> p m n", n=NST),
                                    in0, in1, ALU.mult), "V")

            @block.scalar
            def _(s, dc=dc, mh=mh):
                s.wait_ge(sems["V"], cur("V"))
                inc(s.activation(prodbuf, prodbuf, AF.Exp), "S")

            @block.vector
            def _(v, dc=dc, mh=mh):
                v.wait_ge(sems["S"], cur("S"))
                ct = Ct_v[:, dc * NST:(dc + 1) * NST]
                pb3 = prodbuf.rearrange("p (m n) -> p m n", n=NST)
                inc(v.tensor_tensor(pb3, pb3, ct.unsqueeze(1).broadcast_to([128, 32, NST]),
                                    ALU.mult), "V")
                v.drain()
                inc(v.tensor_reduce(t["k_loc"][:, dc * Q + mh * 32: dc * Q + (mh + 1) * 32],
                                    pb3, AX.X, ALU.add), "V")
                v.drain()
        M["KLOC_V"] = cur("V")

        @block.sync
        def _(sy):
            sy.wait_ge(sems["V"], M["KLOC_V"])
            zf = t["zeros_c"].rearrange("a b -> (a b)")
            ksf = t["kshift_dram"].rearrange("j d i -> (j d i)")
            NZ = 128 * 4096
            for z in range(Q * D * Q // NZ):
                inc(sy.dma_start(out=ksf[z * NZ:(z + 1) * NZ], in_=zf), "LDX", 16)
            sy.wait_ge(sems["LDX"], cur("LDX"))
            for j in range(Q):
                dst = t["kshift_dram"][j].rearrange("(dc p) i -> p dc i", p=128)
                inc(sy.dma_start(out=dst[:, :, j:Q],
                                 in_=t["k_loc"][:].rearrange("p (dc m) -> p dc m", m=Q)[:, :, 0:Q - j],
                                 allow_slow_non_contiguous=True), "LDX", 16)
        M["KSHIFT_D"] = cur("LDX")

        cp_evac = {}
        for dc in range(NDC):
            @block.tensor
            def _(tt, dc=dc):
                if dc == 0:
                    tt.wait_ge(sems["V"], M["KLOC_V"])
                    tt.wait_ge(sems["LDX"], M["LDX_SMALL"])
                else:
                    tt.wait_ge(sems["V"], cp_evac[dc - 1])
                inc(tt.transpose(ps_win[0][0:64, 0:128], logdA_v[:, dc * NST:(dc + 1) * NST],
                                 t["ident_f"][:]), "T")
                inc(tt.transpose(ps_win[1][0:64, 0:128], Ct_v[:, dc * NST:(dc + 1) * NST],
                                 t["ident_f"][:]), "T")

            @block.vector
            def _(v, dc=dc):
                v.wait_ge(sems["T"], cur("T"))
                for (dst, bank) in ((t["logdA_P2"], 0), (t["Ct_P2"], 1)):
                    src = ps_win[bank][0:64, 0:128].rearrange("n (dd two) -> n dd two", two=2)
                    inc(v.tensor_copy(dst[0:64, dc * 64:(dc + 1) * 64], src[:, :, 0]), "V")
                    inc(v.tensor_copy(dst[64:128, dc * 64:(dc + 1) * 64], src[:, :, 1]), "V")
                cp_evac[dc] = cur("V")
        M["P2_V"] = cur("V")

        @block.scalar
        def _(s):
            s.wait_ge(sems["V"], M["P2_V"])
            inc(s.activation(t["r64_P2"][:], t["logdA_P2"][:], AF.Exp, scale=64.0), "S")
        M["R64_S"] = cur("S")

    if stop >= PHASES.index("bloc"):
        ps_pb_t = es.enter_context(nc.psum_tensor("ps_pb", [128, 1024], F32))
        ps_pbe = ps_pb_t[:, 0:512]
        ps_pbo = ps_pb_t[:, 512:1024]
        ps_ys_t = es.enter_context(nc.psum_tensor("ps_ys", [128, 1024], F32))
        ps_ys = [ps_ys_t[:, 0:512], ps_ys_t[:, 512:1024]]

        pb_done = {}
        pb_used = {}
        ctp_used = {}
        ys_half = {}
        tload_mark = {}
        tload_used = {}
        ps_trf = ps_tr_t.bitcast(F32)

        for g in range(8):
            for hh in range(4):
                @block.gpsimd
                def _(gp, g=g, hh=hh):
                    gp.wait_ge(sems["LDX"], M["LOGDA_D"])
                    if g == 0 and hh == 0:
                        gp.wait_ge(sems["V"], M["KLOC_V"])
                    elif hh == 0:
                        gp.wait_ge(sems["V"], M[("CTP_V", g - 1)])
                    else:
                        gp.wait_ge(sems["S"], cur("S"))
                    src = t["logdA_dram"].rearrange("d n -> (d n)")[
                        (g * 128 + 32 * hh) * 64:(g * 128 + 32 * hh + 32) * 64]
                    src = src.unsqueeze(0).broadcast_to([128, 2048])
                    inc(gp.dma_start(out=prodbuf, in_=src), "G", 16)

                @block.vector
                def _(v, g=g, hh=hh):
                    v.wait_ge(sems["G"], cur("G"))
                    inc(v.tensor_scalar(prodbuf, prodbuf, JNEG, None, ALU.mult, ALU.bypass), "V")
                M[("PBJ", g, hh)] = cur("V")

                @block.scalar
                def _(s, g=g, hh=hh):
                    s.wait_ge(sems["V"], M[("PBJ", g, hh)])
                    if g >= 1 and hh == 0:
                        s.wait_ge(sems["T"], pb_used[g - 1])
                    inc(s.activation(Pb_g[:, 2048 * hh:2048 * hh + 2048], prodbuf, AF.Exp), "S")
            M[("PB_S", g)] = cur("S")

            for ch in range(2):
                @block.vector
                def _(v, g=g, ch=ch):
                    v.wait_ge(sems["S"], M[("PB_S", g)] if ch == 0 else cur("S"))
                    v.wait_ge(sems["S"], M["R64_S"])
                    if g >= 1 and ch == 0:
                        v.wait_ge(sems["T"], ctp_used[g - 1])
                    ld = t["logdA_P2"][:, g * 64 + 32 * ch: g * 64 + 32 * ch + 32]
                    in0 = ld.unsqueeze(2).broadcast_to([128, 32, 64])
                    in1 = t["iramp1"][:].unsqueeze(1).broadcast_to([128, 32, 64])
                    inc(v.tensor_tensor(prodbuf.rearrange("p (pr i) -> p pr i", i=64),
                                        in0, in1, ALU.mult), "V")

                @block.scalar
                def _(s, g=g, ch=ch):
                    s.wait_ge(sems["V"], cur("V"))
                    inc(s.activation(prodbuf, prodbuf, AF.Exp), "S")

                @block.vector
                def _(v, g=g, ch=ch):
                    v.wait_ge(sems["S"], cur("S"))
                    ct = t["Ct_P2"][:, g * 64 + 32 * ch: g * 64 + 32 * ch + 32]
                    inc(v.tensor_tensor(CtP_g[:, 2048 * ch:2048 * ch + 2048].rearrange(
                                            "p (pr i) -> p pr i", i=64),
                                        prodbuf.rearrange("p (pr i) -> p pr i", i=64),
                                        ct.unsqueeze(2).broadcast_to([128, 32, 64]), ALU.mult), "V")
            M[("CTP_V", g)] = cur("V")

            for qq in range(4):
                @block.tensor
                def _(tt, g=g, qq=qq):
                    tt.wait_ge(sems["S"], M[("PB_S", g)])
                    tt.wait_ge(sems["V"], M["XPT_MASKED"])
                    if (g, qq) != (0, 0):
                        prev = (g, qq - 1) if qq > 0 else (g - 1, 3)
                        tt.wait_ge(sems["V"], pb_done[prev])
                    if qq == 0 and g > 0:
                        tt.wait_ge(sems["V"], ys_half[(g - 1, 1)])
                    xv = xpT.rearrange("p (c d) -> p c d", d=D)
                    for pr in range(16):
                        for dpar in range(2):
                            din = qq * 32 + pr * 2 + dpar
                            d = g * 128 + din
                            inc(tt.matmul(ps_pbe[64 * dpar:64 * dpar + 64, pr * 32:(pr + 1) * 32],
                                          Pb_g[0:64, din * 64:(din + 1) * 64],
                                          xv[0:64, :, d], start=True, stop=True,
                                          tile_position=(0, 64 * dpar)), "T")
                            inc(tt.matmul(ps_pbo[64 * dpar:64 * dpar + 64, pr * 32:(pr + 1) * 32],
                                          Pb_g[64:128, din * 64:(din + 1) * 64],
                                          xv[64:128, :, d], start=True, stop=True,
                                          tile_position=(64, 64 * dpar)), "T")
                    if qq == 3:
                        pb_used[g] = cur("T")

                @block.vector
                def _(v, g=g, qq=qq):
                    v.wait_ge(sems["T"], cur("T"))
                    b3 = B_sb.rearrange("p (pr c two) -> p pr c two", c=32, two=2)
                    inc(v.tensor_copy(b3[:, :, :, 0],
                                      ps_pbe.rearrange("p (pr c) -> p pr c", c=32)), "V")
                    inc(v.tensor_copy(b3[:, :, :, 1],
                                      ps_pbo.rearrange("p (pr c) -> p pr c", c=32)), "V")
                    pb_done[(g, qq)] = cur("V")
                    v.drain()
                    for pr in range(16):
                        prg = qq * 16 + pr
                        inc(v.tensor_tensor_scan(
                            S_g[:, prg * 64:(prg + 1) * 64],
                            t["r64_P2"][:, g * 64 + prg:g * 64 + prg + 1].broadcast_to([128, 64]),
                            B_sb[:, pr * 64:(pr + 1) * 64],
                            0.0, ALU.mult, ALU.add), "V")
            M[("SCAN_V", g)] = cur("V")

            for half in range(2):
                hslot = (2 * g + half) % 2

                @block.gpsimd
                def _(gp, g=g, half=half, hslot=hslot):
                    gp.wait_ge(sems["LDX"], M["KSHIFT_D"])
                    gp.wait_ge(sems["S"], M["R64_S"])
                    k2 = (g, half)
                    prev = tload_used.get((2 * g + half) - 2)
                    if prev is not None:
                        gp.wait_ge(sems["T"], prev)
                    src = t["kshift_dram"][:, g * 128 + 64 * half: g * 128 + 64 * half + 64, :]
                    dst = T_bf(hslot).rearrange("p (dh i) -> p dh i", i=Q)
                    inc(gp.dma_start(out=dst[0:64], in_=src), "LDW", 16)
                    inc(gp.dma_start(out=dst[64:128], in_=src), "LDW", 16)
                    tload_mark[(g, half)] = cur("LDW")

                @block.tensor
                def _(tt, g=g, half=half, hslot=hslot):
                    tt.wait_ge(sems["V"], M[("SCAN_V", g)])
                    tt.wait_ge(sems["V"], M[("CTP_V", g)])
                    tt.wait_ge(sems["LDW"], tload_mark[(g, half)])
                    if (g, half) != (0, 0):
                        prev = (g, 0) if half == 1 else (g - 1, 1)
                        tt.wait_ge(sems["V"], ys_half[prev])
                    xv = xpT.rearrange("p (c d) -> p c d", d=D)
                    for prh in range(32):
                        pr = half * 32 + prh
                        for dpar in range(2):
                            din = pr * 2 + dpar
                            d = g * 128 + din
                            dh = din - 64 * half
                            lhsT = CtP_g[64 * dpar:64 * dpar + 64, pr * 64:(pr + 1) * 64]
                            Sr = S_g[64 * dpar:64 * dpar + 64, pr * 64:(pr + 1) * 64]
                            mve = Sr[:, 31:63].rearrange("n (c two) -> n c two", two=2)[:, :, 0]
                            mvo = Sr[:, 32:64].rearrange("n (c two) -> n c two", two=2)[:, :, 0]
                            inc(tt.matmul(ps_ys[dpar][0:64, prh * 16:(prh + 1) * 16],
                                          lhsT, mve, start=True, stop=True,
                                          tile_position=(64 * dpar, 0)), "T")
                            inc(tt.matmul(ps_win[dpar][64:128, prh * 16:(prh + 1) * 16],
                                          lhsT, mvo, start=True, stop=True,
                                          tile_position=(64 * dpar, 64)), "T")
                            Tst0 = T_bf(hslot)[0:64, dh * 64:(dh + 1) * 64]
                            Tst1 = T_bf(hslot)[64:128, dh * 64:(dh + 1) * 64]
                            lbank_e = ps_pbe if dpar == 0 else ps_pbo
                            lbank_o = ps_trf[:, 0:512] if dpar == 0 else ps_trf[:, 512:1024]
                            inc(tt.matmul(lbank_e[0:64, prh * 16:(prh + 1) * 16],
                                          Tst0, xv[0:64, 16:32, d], start=True, stop=True,
                                          tile_position=(0, 0)), "T")
                            inc(tt.matmul(lbank_o[64:128, prh * 16:(prh + 1) * 16],
                                          Tst1, xv[64:128, 16:32, d], start=True, stop=True,
                                          tile_position=(64, 64)), "T")
                    if half == 1:
                        ctp_used[g] = cur("T")
                    tload_used[2 * g + half] = cur("T")

                @block.vector
                def _(v, g=g, half=half):
                    v.wait_ge(sems["T"], cur("T"))
                    for dpar in range(2):
                        d0 = g * 128 + 64 * half + dpar
                        dhi0 = d0 // 2
                        yte = y_T[0:64, :].rearrange("r (c dhi two) -> r dhi two c",
                                                     two=2, dhi=512)[:, dhi0:dhi0 + 32, dpar, :]
                        srce = ps_ys[dpar][0:64, :].rearrange("r (s c) -> r s c", c=16)
                        inc(v.tensor_copy(yte, srce), "V")
                        yto = y_T[64:128, :].rearrange("r (c dhi two) -> r dhi two c",
                                                       two=2, dhi=512)[:, dhi0:dhi0 + 32, dpar, :]
                        srco = ps_win[dpar][64:128, :].rearrange("r (s c) -> r s c", c=16)
                        inc(v.tensor_copy(yto, srco), "V")
                        v.drain()
                        lbank_e = ps_pbe if dpar == 0 else ps_pbo
                        lbank_o = ps_trf[:, 0:512] if dpar == 0 else ps_trf[:, 512:1024]
                        inc(v.tensor_tensor(yte, yte,
                                            lbank_e[0:64, :].rearrange("r (s c) -> r s c", c=16),
                                            ALU.add), "V")
                        inc(v.tensor_tensor(yto, yto,
                                            lbank_o[64:128, :].rearrange("r (s c) -> r s c", c=16),
                                            ALU.add), "V")
                        v.drain()
                    ys_half[(g, half)] = cur("V")
        M["YT_DONE"] = cur("V")
        M["BLOC_T"] = cur("T")

    if stop >= PHASES.index("local"):
        xb_evac = {}
        for c in range(16):
            for db in range(NDC):
                k = c * NDC + db
                bank = k % 2

                @block.tensor
                def _(tt, c=c, db=db, k=k, bank=bank):
                    tt.wait_ge(sems["T"], M["BLOC_T"])
                    tt.wait_ge(sems["V"], M["YT_DONE"])
                    if k >= 2:
                        tt.wait_ge(sems["V"], xb_evac[k - 2])
                    inc(tt.transpose(ps_tr[bank][:, 0:128],
                                     xpT[:, (16 + c) * 1024 + db * 128:(16 + c) * 1024 + (db + 1) * 128],
                                     t["ident_bf"][:]), "T")

                @block.vector
                def _(v, c=c, db=db, k=k, bank=bank):
                    v.wait_ge(sems["T"], cur("T"))
                    inc(v.tensor_copy(xpdt[:, db * 2048 + c * 128: db * 2048 + (c + 1) * 128],
                                      ps_tr[bank][:, 0:128]), "V")
                    xb_evac[k] = cur("V")
        M["XPDT_V"] = cur("V")

        ytb_evac = {}
        for dc in range(NDC):
            @block.vector
            def _(v, dc=dc):
                v.wait_ge(sems["V"], M["XPDT_V"])
                xsl = xpdt[:, dc * 2048:(dc + 1) * 2048]
                inc(v.tensor_scalar(y_dc, xsl, pv("Dp", dc), None, ALU.mult, ALU.bypass), "V")
                v.drain()
            M[("TAPS", dc)] = cur("V")

            for c in range(16):
                bank = c % 2

                @block.tensor
                def _(tt, dc=dc, c=c, bank=bank):
                    tt.wait_ge(sems["V"], M[("TAPS", dc)])
                    key = (dc, c - 2) if c >= 2 else ((dc - 1, c + 14) if dc > 0 else None)
                    if key is not None:
                        tt.wait_ge(sems["V"], ytb_evac[key])
                    inc(tt.transpose(ps_tr[bank][:, 128:256],
                                     y_T[:, c * 1024 + dc * 128: c * 1024 + (dc + 1) * 128],
                                     t["ident_bf"][:]), "T")

                @block.vector
                def _(v, dc=dc, c=c, bank=bank):
                    v.wait_ge(sems["T"], cur("T"))
                    dst = y_dc[:, c * 128:(c + 1) * 128]
                    inc(v.tensor_tensor(dst, dst, ps_tr[bank][:, 128:256], ALU.add), "V")
                    ytb_evac[(dc, c)] = cur("V")

            @block.vector
            def _(v, dc=dc):
                v.drain()
                xsl = xpdt[:, dc * 2048:(dc + 1) * 2048]
                inc(v.tensor_copy(xsl, y_dc), "V")
                v.drain()
        M["Y_V"] = cur("V")

    if stop >= PHASES.index("router"):
        @block.sync
        def _(sy):
            sy.wait_ge(sems["T"], cur("T"))
            sy.wait_ge(sems["V"], M["Y_V"])
            inc(sy.dma_start(out=acc.rearrange("p (dc k) -> p dc k", k=TOWN),
                             in_=t["xoT"].rearrange("(dc p) k -> p dc k", p=128)), "LDX", 16)
        M["ACC_LD"] = cur("LDX")

        @block.tensor
        def _(tt):
            tt.wait_ge(sems["LDX"], M["ACC_LD"])
            for dc in range(NDC):
                inc(tt.matmul(ps_win[0][0:8, 0:1], t["WrT_sb"][:, dc * E:(dc + 1) * E],
                              ONES, start=(dc == 0), stop=(dc == NDC - 1)), "T")

        @block.vector
        def _(v):
            v.wait_ge(sems["T"], cur("T"))
            inc(v.tensor_copy(t["wsum_sb"][:], ps_win[0][0:8, 0:1]), "V")
            v.drain()
            inc(v.tensor_scalar(t["wsum_diag"][:], t["ident8"][:], t["wsum_sb"][:, 0:1], None,
                                ALU.mult, ALU.bypass), "V")
        M["WSUMD"] = cur("V")

        @block.tensor
        def _(tt):
            tt.wait_ge(sems["V"], M["WSUMD"])
            inc(tt.matmul(ps_win[0][:, 8:16], t["ones8"][:], t["wsum_diag"][:],
                          start=True, stop=True), "T")

        @block.vector
        def _(v):
            v.wait_ge(sems["T"], cur("T"))
            inc(v.tensor_copy(t["wsum_bc"][:], ps_win[0][:, 8:16]), "V")
        M["WSUM_BC"] = cur("V")

        lr_evac = {}
        for tt4 in range(4):
            @block.tensor
            def _(tt, tt4=tt4):
                if tt4 >= 1:
                    tt.wait_ge(sems["V"], lr_evac[tt4 - 1])
                else:
                    tt.wait_ge(sems["V"], M["WSUM_BC"])
                for dc in range(NDC):
                    mv = acc[:, dc * 2048 + tt4 * 512: dc * 2048 + tt4 * 512 + 512]
                    inc(tt.matmul(ps_win[1][0:8, 0:512], t["WrT_sb"][:, dc * E:(dc + 1) * E],
                                  mv, start=(dc == 0), stop=(dc == NDC - 1)), "T")

            @block.vector
            def _(v, tt4=tt4):
                v.wait_ge(sems["T"], cur("T"))
                inc(v.tensor_copy(lraw[:, tt4 * 512:(tt4 + 1) * 512], ps_win[1][0:8, 0:512]), "V")
                lr_evac[tt4] = cur("V")
        M["LRAW"] = cur("V")

        rt_evac = {}
        for tk in range(16):
            bank = tk % 2

            @block.tensor
            def _(tt, tk=tk, bank=bank):
                tt.wait_ge(sems["V"], M["LRAW"])
                if tk >= 2:
                    tt.wait_ge(sems["V"], rt_evac[tk - 2])
                inc(tt.transpose(ps_win[bank][0:128, 0:8],
                                 lraw[:, tk * 128:(tk + 1) * 128], t["ident8"][:]), "T")

            @block.vector
            def _(v, tk=tk, bank=bank):
                v.wait_ge(sems["T"], cur("T"))
                inc(v.tensor_copy(raw_tok[:, tk * 8:(tk + 1) * 8], ps_win[bank][0:128, 0:8]), "V")
                rt_evac[tk] = cur("V")

        r3 = lambda x: x.rearrange("p (k e) -> p k e", e=8)
        rmv = lambda a: rm_v[:, a * 16:(a + 1) * 16]
        bc8 = lambda x: x.unsqueeze(2).broadcast_to([128, 16, 8])

        @block.vector
        def _(v):
            v.wait_ge(sems["V"], cur("V"))
            rstd_own = t["rstd_all"][:, 16:32]
            mean_own = t["mvall"][:].rearrange("p (k two) -> p k two", two=2)[:, 16:32, 0]
            inc(v.tensor_tensor(rmv(0), mean_own, rstd_own, ALU.mult), "V")
            v.drain()
            inc(v.tensor_tensor(r3(nl_v), r3(raw_tok), bc8(rstd_own), ALU.mult), "V")
            v.drain()
            wb = t["wsum_bc"][:].unsqueeze(1).broadcast_to([128, 16, 8])
            inc(v.tensor_tensor(r3(t2_v), wb, bc8(rmv(0)), ALU.mult), "V")
            v.drain()
            inc(v.tensor_tensor(r3(nl_v), r3(t2_v), r3(nl_v), ALU.subtract), "V")
            v.drain()
            inc(v.tensor_reduce(rmv(1), r3(nl_v), AX.X, ALU.min), "V")
            v.drain()
            inc(v.tensor_tensor(r3(ez_v), bc8(rmv(1)), r3(nl_v), ALU.subtract), "V")
        M["RZ"] = cur("V")

        @block.scalar
        def _(s):
            s.wait_ge(sems["V"], M["RZ"])
            inc(s.activation(ez_v, ez_v, AF.Exp), "S")
        M["REZ"] = cur("S")

        @block.vector
        def _(v):
            v.wait_ge(sems["S"], M["REZ"])
            inc(v.tensor_reduce(rmv(4), r3(ez_v), AX.X, ALU.add), "V")
            inc(v.tensor_reduce(rmv(2), r3(ez_v), AX.X, ALU.max), "V")
            v.drain()
            inc(v.tensor_tensor(r3(msk1_v), r3(ez_v), bc8(rmv(2)), ALU.subtract), "V")
            v.drain()
            inc(v.tensor_scalar(msk1_v, msk1_v, 0.0, None, ALU.is_ge, ALU.bypass), "V")
            v.drain()
            inc(v.scalar_tensor_tensor(s2_v, msk1_v, 2.0, ez_v, ALU.mult, ALU.subtract), "V")
            v.drain()
            inc(v.tensor_reduce(rmv(3), r3(s2_v), AX.X, ALU.min), "V")
            v.drain()
            inc(v.tensor_tensor(r3(msk2_v), r3(s2_v), bc8(rmv(3)), ALU.subtract), "V")
            v.drain()
            inc(v.tensor_scalar(msk2_v, msk2_v, 0.0, None, ALU.is_le, ALU.bypass), "V")
            v.drain()
            inc(v.scalar_tensor_tensor(rmv(5), rmv(4), EPS_TOP, rmv(2), ALU.mult, ALU.add), "V")
            v.drain()
            inc(v.tensor_tensor(rmv(5), rmv(5), rmv(3), ALU.subtract), "V")
            v.drain()
            inc(v.reciprocal(rmv(5), rmv(5)), "V")
            v.drain()
            inc(v.tensor_tensor(r3(comb_v), r3(msk1_v), bc8(rmv(2)), ALU.mult), "V")
            inc(v.tensor_tensor(r3(t2_v), r3(msk2_v), bc8(rmv(3)), ALU.mult), "V")
            v.drain()
            inc(v.tensor_tensor(comb_v, comb_v, t2_v, ALU.subtract), "V")
            v.drain()
            inc(v.tensor_tensor(r3(comb_v), r3(comb_v), bc8(rmv(5)), ALU.mult), "V")
        M["COMB"] = cur("V")

        ct_evac = {}
        for tk in range(16):
            bank = tk % 2

            @block.tensor
            def _(tt, tk=tk, bank=bank):
                tt.wait_ge(sems["V"], M["COMB"])
                if tk >= 2:
                    tt.wait_ge(sems["V"], ct_evac[tk - 2])
                inc(tt.transpose(ps_win[bank][0:8, 0:128],
                                 comb_v[:, tk * 8:(tk + 1) * 8], t["ident_f"][:]), "T")

            @block.vector
            def _(v, tk=tk, bank=bank):
                v.wait_ge(sems["T"], cur("T"))
                inc(v.tensor_copy(t["comb_T"][:, tk * 128:(tk + 1) * 128],
                                  ps_win[bank][0:8, 0:128]), "V")
                ct_evac[tk] = cur("V")
        M["COMB_T"] = cur("V")

    if stop >= PHASES.index("wout"):
        @block.gpsimd
        def _(g):
            g.wait_ge(sems["T"], M["BLOC_T"])
            g.wait_ge(sems["V"], M["Y_V"])
            inc(g.dma_start(out=Wout_bf.rearrange("p (dc o) -> p dc o", o=D),
                            in_=t["WoutT"].rearrange("(dc p) o -> p dc o", p=128)), "LDW", 16)
        M["WOUT_LD"] = cur("LDW")

        wo_evac = {}
        for tt4 in range(4):
            for doutc in range(NDC):
                k = tt4 * NDC + doutc
                bank = k % 2

                @block.tensor
                def _(tt, tt4=tt4, doutc=doutc, k=k, bank=bank):
                    tt.wait_ge(sems["LDW"], M["WOUT_LD"])
                    tt.wait_ge(sems["V"], M["Y_V"])
                    tt.wait_ge(sems["LDX"], M["ACC_LD"])
                    if k >= 2:
                        tt.wait_ge(sems["V"], wo_evac[k - 2])
                    for dc in range(NDC):
                        mv = xpdt[:, dc * 2048 + tt4 * 512: dc * 2048 + tt4 * 512 + 512]
                        inc(tt.matmul(ps_win[bank],
                                      Wout_bf[:, dc * D + doutc * 128: dc * D + (doutc + 1) * 128],
                                      mv, start=(dc == 0), stop=(dc == NDC - 1)), "T")

                @block.vector
                def _(v, tt4=tt4, doutc=doutc, k=k, bank=bank):
                    v.wait_ge(sems["T"], cur("T"))
                    a = acc[:, doutc * 2048 + tt4 * 512: doutc * 2048 + tt4 * 512 + 512]
                    inc(v.scalar_tensor_tensor(a, ps_win[bank], pv("bout", doutc),
                                               a, ALU.add, ALU.add), "V")
                    wo_evac[k] = cur("V")
        M["WOUT_V"] = cur("V")
        M["WOUT_T"] = cur("T")

    if stop >= PHASES.index("moe"):
        @block.gpsimd
        def _(gp):
            gp.wait_ge(sems["V"], M["COMB_T"])
            gp.wait_ge(sems["T"], M["WOUT_T"])
            inc(gp.dma_start(out=sel_bf, in_=t["sel_c"]), "LDW", 16)
        M["SEL_LD"] = cur("LDW")

        ps_trf = ps_tr_t.bitcast(F32)
        BSET = [[ps_win[0], ps_win[1], ps_trf[:, 0:512], ps_trf[:, 512:1024]],
                [ps_ys[0], ps_ys[1], ps_pbe, ps_pbo]]

        wg_used = {}
        wd_used = {}
        hwaitv = {}
        h_done = {}
        gr_done = {}
        cb_evac = {}
        mo_evac = {}

        for e in range(E):
            for tt4 in range(4):
                @block.tensor
                def _(tt, e=e, tt4=tt4):
                    tt.wait_ge(sems["LDW"], M["SEL_LD"])
                    if (e, tt4) != (0, 0):
                        prev = (e, tt4 - 1) if tt4 > 0 else (e - 1, 3)
                        tt.wait_ge(sems["V"], cb_evac[prev])
                    if e > 0:
                        tt.wait_ge(sems["V"], M[("A_DONE", e - 1, 3)])
                        tt.wait_ge(sems["V"], mo_evac[(e - 1, 3, 15)])
                    inc(tt.matmul(ps_pbo[:, 0:512], sel_bf[:, e * 128:(e + 1) * 128],
                                  t["comb_T"][:, tt4 * 512:(tt4 + 1) * 512],
                                  start=True, stop=True), "T")

                @block.vector
                def _(v, e=e, tt4=tt4):
                    v.wait_ge(sems["T"], cur("T"))
                    inc(v.tensor_copy(comb_bc[:, tt4 * 512:(tt4 + 1) * 512], ps_pbo[:, 0:512]), "V")
                    cb_evac[(e, tt4)] = cur("V")
            M[("CB", e)] = cur("V")

            for qtr in range(4):
                for sub in range(2):
                    ei = qtr * 2 + sub
                    slot = ei % 2

                    @block.gpsimd
                    def _(gp, e=e, ei=ei, slot=slot):
                        if e == 0 and ei < 2:
                            gp.wait_ge(sems["T"], M["WOUT_T"])
                            gp.wait_ge(sems["V"], M["WOUT_V"])
                        key = (e, ei - 2) if ei >= 2 else ((e - 1, 6 + ei) if e > 0 else None)
                        if key is not None:
                            gp.wait_ge(sems["T"], wg_used[key])
                        src = t["WgT"][e, :, ei * 512:(ei + 1) * 512]
                        inc(gp.dma_start(out=Wg_bf(slot).rearrange("p (dc f) -> p dc f", f=512),
                                         in_=src.rearrange("(dc p) f -> p dc f", p=128)), "LDW", 16)
                        src2 = t["WuT"][e, :, ei * 512:(ei + 1) * 512]
                        inc(gp.dma_start(out=Wu_bf(slot).rearrange("p (dc f) -> p dc f", f=512),
                                         in_=src2.rearrange("(dc p) f -> p dc f", p=128)), "LDW", 16)
                    M[("WGDMA", e, ei)] = cur("LDW")

                @block.gpsimd
                def _(gp, e=e, qtr=qtr):
                    if e == 0 and qtr == 0:
                        gp.wait_ge(sems["T"], M["WOUT_T"])
                        gp.wait_ge(sems["V"], M["WOUT_V"])
                    key = (e, qtr - 1) if qtr >= 1 else ((e - 1, 3) if e > 0 else None)
                    if key is not None:
                        gp.wait_ge(sems["T"], wd_used[key])
                    src = t["WdT"][e, qtr * 1024:(qtr + 1) * 1024, :]
                    inc(gp.dma_start(out=Wd_bf.rearrange("p (q o) -> p q o", o=D),
                                     in_=src.rearrange("(q p) o -> p q o", p=128)), "LDW", 16)
                M[("WDDMA", e, qtr)] = cur("LDW")

                for sub in range(2):
                    ei = qtr * 2 + sub
                    slot = ei % 2
                    for kk2 in range(8):
                        tt2 = kk2 // 4
                        dffc = kk2 % 4
                        par = kk2 % 2
                        dffq = sub * 4 + dffc
                        SB_ = BSET[par]

                        @block.tensor
                        def _(tt, e=e, qtr=qtr, ei=ei, slot=slot, kk2=kk2, tt2=tt2, dffc=dffc, par=par, SB_=SB_):
                            tt.wait_ge(sems["LDW"], M[("WGDMA", e, ei)])
                            if ei == 0 and kk2 <= 1:
                                tt.wait_ge(sems["V"], M[("CB", e)])
                            if kk2 >= 2:
                                tt.wait_ge(sems["V"], h_done[(e, ei, kk2 - 2)])
                            elif ei % 2 == 1:
                                tt.wait_ge(sems["V"], h_done[(e, ei - 1, 6 + kk2)])
                            else:
                                pq = (e, qtr - 1) if qtr > 0 else ((e - 1, 3) if e > 0 else None)
                                if pq is not None:
                                    tt.wait_ge(sems["V"], mo_evac[(pq[0], pq[1], 14 + kk2)])
                            for tb in range(2):
                                W = Wg_bf(slot) if tb == 0 else Wu_bf(slot)
                                for dc in range(NDC):
                                    st0 = W[:, dc * 512 + dffc * 128: dc * 512 + (dffc + 1) * 128]
                                    for j in range(2):
                                        tt4 = 2 * tt2 + j
                                        mv = t["xn_T"][:, dc * 2048 + tt4 * 512: dc * 2048 + tt4 * 512 + 512]
                                        inc(tt.matmul(SB_[2 * tb + j], st0, mv,
                                                      start=(dc == 0), stop=(dc == NDC - 1)), "T")
                            if kk2 == 7:
                                wg_used[(e, ei)] = cur("T")
                        hwaitv[(e, ei, kk2)] = None

                        @block.scalar
                        def _(s, e=e, ei=ei, kk2=kk2, par=par, SB_=SB_):
                            s.wait_ge(sems["T"], cur("T") - 16)
                            if kk2 >= 2:
                                s.wait_ge(sems["V"], h_done[(e, ei, kk2 - 2)])
                            inc(s.activation(gr_sb(2 * par), SB_[0], AF.Relu), "S")
                            inc(s.activation(gr_sb(2 * par + 1), SB_[1], AF.Relu), "S")
                            gr_done[(e, ei, kk2)] = cur("S")

                        @block.vector
                        def _(v, e=e, ei=ei, kk2=kk2, tt2=tt2, par=par, SB_=SB_, dffq=dffq):
                            v.wait_ge(sems["T"], cur("T"))
                            v.wait_ge(sems["S"], gr_done[(e, ei, kk2)])
                            h0 = h_all[:, (dffq * 4 + 2 * tt2) * 512:(dffq * 4 + 2 * tt2) * 512 + 512]
                            h1 = h_all[:, (dffq * 4 + 2 * tt2 + 1) * 512:(dffq * 4 + 2 * tt2 + 1) * 512 + 512]
                            inc(v.tensor_tensor(h0, gr_sb(2 * par), SB_[2], ALU.mult), "V")
                            inc(v.tensor_tensor(h1, gr_sb(2 * par + 1), SB_[3], ALU.mult), "V")
                            v.drain()
                            inc(v.tensor_tensor(h0, h0, comb_bc[:, 2 * tt2 * 512:(2 * tt2 + 1) * 512], ALU.mult), "V")
                            inc(v.tensor_tensor(h1, h1, comb_bc[:, (2 * tt2 + 1) * 512:(2 * tt2 + 2) * 512], ALU.mult), "V")
                            h_done[(e, ei, kk2)] = cur("V")
                M[("A_DONE", e, qtr)] = cur("V")

                for pk in range(16):
                    tt2 = pk // 8
                    doutc = pk % 8
                    Bset = BSET[pk % 2]

                    @block.tensor
                    def _(tt, e=e, qtr=qtr, pk=pk, tt2=tt2, doutc=doutc, Bset=Bset):
                        tt.wait_ge(sems["LDW"], M[("WDDMA", e, qtr)])
                        tt.wait_ge(sems["V"], M[("A_DONE", e, qtr)])
                        key = (e, qtr, pk - 2)
                        if key in mo_evac:
                            tt.wait_ge(sems["V"], mo_evac[key])
                        else:
                            pko = (e, qtr - 1, 14 + pk) if qtr > 0 else \
                                  ((e - 1, 3, 14 + pk) if e > 0 else None)
                            if pko is not None:
                                tt.wait_ge(sems["V"], mo_evac[pko])
                        for dffq in range(8):
                            st0 = Wd_bf[:, dffq * D + doutc * 128: dffq * D + (doutc + 1) * 128]
                            for j in range(2):
                                tt4 = 2 * tt2 + j
                                mv = h_all[:, (dffq * 4 + tt4) * 512:(dffq * 4 + tt4) * 512 + 512]
                                inc(tt.matmul(Bset[2 * j], st0, mv,
                                              start=(dffq == 0), stop=(dffq == 7)), "T")
                        if pk == 15:
                            wd_used[(e, qtr)] = cur("T")

                    @block.vector
                    def _(v, e=e, qtr=qtr, pk=pk, tt2=tt2, doutc=doutc, Bset=Bset):
                        v.wait_ge(sems["T"], cur("T"))
                        for j in range(2):
                            tt4 = 2 * tt2 + j
                            a = acc[:, doutc * 2048 + tt4 * 512: doutc * 2048 + tt4 * 512 + 512]
                            inc(v.tensor_tensor(a, a, Bset[2 * j], ALU.add), "V")
                        mo_evac[(e, qtr, pk)] = cur("V")
        M["MOE_V"] = cur("V")

    if stop >= PHASES.index("store"):
        st_evac = {}
        for tk in range(16):
            for dc in range(NDC):
                k = tk * NDC + dc
                bank = k % 2

                @block.tensor
                def _(tt, tk=tk, dc=dc, k=k, bank=bank):
                    tt.wait_ge(sems["V"], M.get("MOE_V", M.get("WOUT_V", cur("V"))))
                    if k >= 2:
                        tt.wait_ge(sems["V"], st_evac[k - 2])
                    inc(tt.transpose(ps_win[bank][:, 0:128],
                                     acc[:, dc * 2048 + tk * 128: dc * 2048 + (tk + 1) * 128],
                                     t["ident_f"][:]), "T")

                @block.vector
                def _(v, tk=tk, dc=dc, k=k, bank=bank):
                    v.wait_ge(sems["T"], cur("T"))
                    if dc == 0 and tk >= 1:
                        v.wait_ge(sems["ST"], 16 * tk)
                    inc(v.tensor_copy(out_stg[:, dc * 128:(dc + 1) * 128],
                                      ps_win[bank][:, 0:128]), "V")
                    st_evac[k] = cur("V")

            @block.sync
            def _(sy, tk=tk):
                sy.wait_ge(sems["V"], st_evac[tk * NDC + NDC - 1])
                inc(sy.dma_start(out=t["out_tok"][tk * 128:(tk + 1) * 128, :], in_=out_stg), "ST", 16)

        @block.sync
        def _(sy):
            sy.wait_ge(sems["ST"], cur("ST"))

    dbg_views = {
        "xn_T": t["xn_T"][:], "xpT": xpT, "xpdt": xpdt, "mvall": t["mvall"][:],
        "rstd_all": t["rstd_all"][:], "k_loc": t["k_loc"][:], "logdA": logdA_v,
        "Ct": Ct_v, "logdA_P2": t["logdA_P2"][:], "Ct_P2": t["Ct_P2"][:],
        "r64_P2": t["r64_P2"][:], "Pb_g": Pb_g, "CtP_g": CtP_g, "S_g": S_g,
        "y_T": y_T, "B_sb": B_sb, "acc": acc, "comb_T": t["comb_T"][:],
        "comb": comb_v, "ez": ez_v, "raw_tok": raw_tok, "lraw": lraw,
        "wsum_bc": t["wsum_bc"][:], "h_all": h_all, "comb_bc": comb_bc,
        "y_dc": y_dc, "dpk0": t["pvec"][:, 32:40], "xn_tok": xn_tok,
    }
    if debug_keys:
        dbg_out = {}
        for name in debug_keys:
            ap = dbg_views[name]
            dbg_out[name] = nc.dram_tensor("dbg_" + name, list(ap.shape), ap.dtype,
                                           kind="ExternalOutput").ap()

        @block.sync
        def _(sy):
            for s in ["V", "S", "T", "G"]:
                sy.wait_ge(sems[s], cur(s))
            for name in debug_keys:
                inc(sy.dma_start(out=dbg_out[name], in_=dbg_views[name]), "ST", 16)
            sy.wait_ge(sems["ST"], cur("ST"))

    es.close()
    return nc



import ml_dtypes
BF = ml_dtypes.bfloat16



def reshape_dvec(v):
    return np.ascontiguousarray(v.reshape(NDC, 128).T)


def reshape_dmat(m):
    K = m.shape[1]
    return np.ascontiguousarray(m.reshape(NDC, 128, K).transpose(1, 0, 2).reshape(128, NDC * K))


def consts():
    c = {}
    c["ident_c"] = np.eye(128, dtype=np.float32)
    sel = np.zeros((8, E * 128), dtype=np.float32)
    for e in range(E):
        sel[e, e * 128:(e + 1) * 128] = 1.0
    c["sel_c"] = sel
    c["ones8_c"] = np.ones((8, 128), dtype=np.float32)
    c["ident8_c"] = np.eye(8, dtype=np.float32)
    smalls = np.zeros((128, 16), dtype=np.float32)
    smalls[:, 0] = 1.0
    smalls[:, 1] = 63.0 - (np.arange(128) % 64)
    c["smalls_c"] = smalls
    c["iramp1_c"] = np.tile(np.arange(1, 65, dtype=np.float32), (128, 1))
    c["mramp_c"] = np.tile(np.arange(64, dtype=np.float32), (128, 1))
    c["zeros_c"] = np.zeros((128, 4096), dtype=np.float32)
    return c


def core_inputs(inputs, core):
    b, h = core // 2, core % 2
    x = np.asarray(inputs["L0_attn_x"])
    own = x[b, h * TOWN:(h + 1) * TOWN, :]
    prev = x[b, 0:TOWN, :] if h == 1 else np.zeros((TOWN, D), np.float32)
    m = {}
    m["xf"] = np.ascontiguousarray(np.concatenate([prev, own], axis=0))
    m["xoT"] = np.ascontiguousarray(own.T)
    m["cmask"] = np.full((128, 1), float(h), np.float32)
    m["WinT"] = np.ascontiguousarray(np.asarray(inputs["L0_attn_Win"]).T)
    m["binb"] = reshape_dvec(np.asarray(inputs["L0_attn_bin_"]))
    m["WoutT"] = np.ascontiguousarray(np.asarray(inputs["L0_attn_Wout"]).T)
    m["boutb"] = reshape_dvec(np.asarray(inputs["L0_attn_bout"]))
    m["A_log_r"] = reshape_dmat(np.asarray(inputs["L0_attn_A_log"]))
    m["B_r"] = reshape_dmat(np.asarray(inputs["L0_attn_B"]))
    m["C_r"] = reshape_dmat(np.asarray(inputs["L0_attn_C"]))
    m["dt_r"] = reshape_dvec(np.asarray(inputs["L0_attn_dt"]))
    m["Dp_r"] = reshape_dvec(np.asarray(inputs["L0_attn_Dp"]))
    m["WrT_r"] = reshape_dmat(np.ascontiguousarray(np.asarray(inputs["L0_ffn_Wr"]).T))
    m["WgT"] = np.ascontiguousarray(np.asarray(inputs["L0_ffn_Wg"]).transpose(0, 2, 1))
    m["WuT"] = np.ascontiguousarray(np.asarray(inputs["L0_ffn_Wu"]).transpose(0, 2, 1))
    m["WdT"] = np.ascontiguousarray(np.asarray(inputs["L0_ffn_Wd"]).transpose(0, 2, 1))
    m.update(consts())
    return m




_CACHE = {}


def kernel(**inputs):
    import numpy as np
    from concourse.bass_utils import run_bass_kernel_spmd
    if "nc" not in _CACHE:
        _CACHE["nc"] = build()
    nc = _CACHE["nc"]
    cores = list(range(8))
    maps = [core_inputs(inputs, c) for c in cores]
    res = run_bass_kernel_spmd(nc, maps, core_ids=cores)
    out = np.zeros((4, TFULL, D), np.float32)
    for ci, c in enumerate(cores):
        b, h = c // 2, c % 2
        out[b, h * TOWN:(h + 1) * TOWN, :] = np.asarray(res.results[ci]["out_tok"])
    return out
